# version 02
# baseline (speedup 1.0000x reference)
import math
import numpy as np
from contextlib import ExitStack

import concourse.bass as bass
import concourse.mybir as mybir
from concourse.bass_utils import run_bass_kernel_spmd

F32 = mybir.dt.float32
BF16 = mybir.dt.bfloat16
I32 = mybir.dt.int32
AF = mybir.ActivationFunctionType
ALU = mybir.AluOpType
AX = mybir.AxisListType

SEQ, T, D, KC, HEADS, DH = 2, 2048, 2048, 16, 16, 128
MCQ, HF, MCF = 48, 5632, 44
TOK, NT128 = SEQ * T, 32
QSCALE = float(1.0 / math.sqrt(DH))
NEG = -1e30
N_W = float(HF * D)

ENGS = ("sync", "gpsimd", "vector", "scalar", "tensor")


def _cp(e, dst, src):
    f = getattr(e, "tensor_copy", None) or e.copy
    return f(dst, src)


class Bld:
    def __init__(self, nc, es):
        self.nc = nc
        self.ev = {e: [] for e in ENGS}
        self.cnt = {}
        self.h = {}
        for name in ("qS", "qG", "pe", "vv", "ss", "gg"):
            self.h[name] = es.enter_context(nc.semaphore(name))
            self.cnt[name] = 0

    def on(self, eng, fn):
        self.ev[eng].append(fn)

    def wait(self, eng, sem, val):
        if not val:
            return
        h = self.h[sem]
        self.on(eng, lambda e, h=h, v=val: e.wait_ge(h, v))

    def waits(self, eng, pairs):
        for sem, val in pairs:
            self.wait(eng, sem, val)

    def op(self, eng, emit, sig=None, n=1):
        if sig is None:
            self.on(eng, emit)
            return None
        self.cnt[sig] += n
        v = self.cnt[sig]
        h = self.h[sig]
        self.on(eng, lambda e, emit=emit, h=h, n=n: emit(e).then_inc(h, n))
        return (sig, v)

    def chain(self, eng, emit):
        sig = {"vector": "vv", "scalar": "ss", "gpsimd": "gg", "tensor": "pe"}[eng]
        r = self.op(eng, emit, sig)
        self.wait(eng, sig, r[1])
        return r

    def dma(self, q, out, in_, waits=()):
        self.waits(q, waits)
        sig = "qS" if q == "sync" else "qG"
        return self.op(q, lambda e, o=out, i=in_: e.dma_start(out=o, in_=i), sig, 16)

    def emit_all(self, blk):
        b = self

        @blk.sync
        def _(e):
            for f in b.ev["sync"]:
                f(e)

        @blk.gpsimd
        def _(e):
            for f in b.ev["gpsimd"]:
                f(e)

        @blk.vector
        def _(e):
            for f in b.ev["vector"]:
                f(e)

        @blk.scalar
        def _(e):
            for f in b.ev["scalar"]:
                f(e)

        @blk.tensor
        def _(e):
            for f in b.ev["tensor"]:
                f(e)


def build_program(debug=False, upto="all"):
    nc = bass.Bass("TRN2", target_bir_lowering=False)
    es = ExitStack()
    dk = "ExternalOutput" if debug else "Internal"

    x_h = nc.dram_tensor("x", [SEQ, T, D], F32, kind="ExternalInput").ap()
    wqkv_h = nc.dram_tensor("wqkv", [MCQ, 128, KC, 128], F32, kind="ExternalInput").ap()
    wout_h = nc.dram_tensor("wout", [D, D], F32, kind="ExternalInput").ap()
    bout_h = nc.dram_tensor("bout", [1, D], F32, kind="ExternalInput").ap()
    w1_h = nc.dram_tensor("w1", [MCF, 128, KC, 128], F32, kind="ExternalInput").ap()
    w2_h = nc.dram_tensor("w2", [MCF, 128, KC, 128], F32, kind="ExternalInput").ap()
    w3_h = nc.dram_tensor("w3", [HF, D], F32, kind="ExternalInput").ap()
    out_h = nc.dram_tensor("out", [SEQ, T, D], F32, kind="ExternalOutput").ap()

    xnT_s = nc.dram_tensor("xnT_s", [KC, 128, TOK], BF16, kind=dk).ap()
    qkv_s = nc.dram_tensor("qkv_s", [MCQ, 8, 128, 512], BF16, kind=dk).ap()
    s_s = nc.dram_tensor("s_s", [NT128, 128, D], F32, kind=dk).ap()
    qw_s = [nc.dram_tensor(f"qw{i}_s", [MCF, 128, D], BF16, kind=dk).ap() for i in (1, 2, 3)]

    identb = es.enter_context(nc.sbuf_tensor("identb", [128, 128], BF16))
    maskT = es.enter_context(nc.sbuf_tensor("maskT", [128, 128], F32))
    ones_cb = es.enter_context(nc.sbuf_tensor("ones_cb", [128, 1], BF16))
    ones_cf = es.enter_context(nc.sbuf_tensor("ones_cf", [128, 1], F32))
    ones_rf = es.enter_context(nc.sbuf_tensor("ones_rf", [1, 128], F32))
    smalls = es.enter_context(nc.sbuf_tensor("smalls", [128, 512], F32))
    rows = es.enter_context(nc.sbuf_tensor("rows", [1, 1536], F32))

    AR = 98304
    arena = es.enter_context(nc.sbuf_tensor("arena", [128, AR], BF16))

    def rbf(off, n):
        return arena[:, off:off + n]

    def rf32(off, nf32):
        return arena[:, off:off + 2 * nf32].bitcast(F32)

    b = Bld(nc, es)

    iotp = rbf(90112, 256).bitcast(I32)
    iotf = rbf(90624, 256).bitcast(I32)
    b.op("gpsimd", lambda e: e.iota(iotf, pattern=[[1, 128]], base=0, channel_multiplier=0))
    g_iot = b.op("gpsimd", lambda e: e.iota(iotp, pattern=[[0, 128]], base=0,
                                            channel_multiplier=1), "gg")
    b.op("gpsimd", lambda e: e.memset(ones_cb[:], 1.0))
    b.op("gpsimd", lambda e: e.memset(ones_cf[:], 1.0))
    g_ones = b.op("gpsimd", lambda e: e.memset(ones_rf[:], 1.0), "gg")
    b.wait("vector", "gg", g_iot[1])
    b.op("vector", lambda e: e.tensor_tensor(identb[:], iotf, iotp, ALU.is_equal))
    b.op("vector", lambda e: e.tensor_tensor(maskT[:], iotp, iotf, ALU.is_gt))
    v_ini = b.op("vector", lambda e: e.tensor_scalar(maskT[:], maskT[:], NEG, None, ALU.mult), "vv")

    o_xnT = 0
    o_xt = [65536, 65536 + 8192]
    o_scrap = 65536 + 16384
    o_xn = [o_scrap + 4096, o_scrap + 6144]
    o_wq = [o_scrap + 8192, o_scrap + 10240]
    o_stage = o_scrap + 12288

    es_ln = ExitStack()
    ln_tp = [es_ln.enter_context(nc.psum_tensor(f"ln_tp{i}", [128, 128], F32)) for i in range(2)]

    xn_done = [None] * NT128
    tp_last = {}
    pe_tp = {}
    for i in range(NT128):
        s, t = divmod(i, 16)
        par = i % 2
        xt = rf32(o_xt[par], 2048)
        xn = rbf(o_xn[par], 2048)
        sm = par * 160
        xld = b.dma("sync", xt, x_h[s, t * 128:(t + 1) * 128, :],
                    waits=[("ss", xn_done[i - 2][1])] if i >= 2 else ())
        b.wait("vector", "qS", xld[1])
        if i >= 2:
            b.wait("vector", "ss", xn_done[i - 2][1])
        b.chain("vector", lambda e, xt=xt, sm=sm: e.reduce_sum(smalls[:, sm:sm + 1], xt, axis=AX.X))
        b.wait("scalar", "qS", xld[1])
        if i == 0:
            b.wait("scalar", "vv", v_ini[1])
        sq = b.op("scalar", lambda e, xt=xt, sm=sm: e.activation(
            rf32(o_scrap, 2048), xt, AF.Square, accum_out=smalls[:, sm + 128:sm + 128 + 1]), "ss")
        b.wait("vector", "ss", sq[1])
        b.chain("vector", lambda e, sm=sm: e.tensor_scalar(
            smalls[:, sm + 16:sm + 16 + 1], smalls[:, sm:sm + 1], -1.0 / D, None, ALU.mult))
        b.chain("vector", lambda e, sm=sm: e.tensor_scalar(
            smalls[:, sm + 32:sm + 32 + 1], smalls[:, sm + 128:sm + 128 + 1], 1.0 / D, None, ALU.mult))
        b.chain("vector", lambda e, sm=sm: e.tensor_tensor(
            smalls[:, sm + 48:sm + 48 + 1], smalls[:, sm + 16:sm + 16 + 1], smalls[:, sm + 16:sm + 16 + 1], ALU.mult))
        b.chain("vector", lambda e, sm=sm: e.tensor_tensor(
            smalls[:, sm + 64:sm + 64 + 1], smalls[:, sm + 32:sm + 32 + 1], smalls[:, sm + 48:sm + 48 + 1],
            ALU.subtract))
        var_c = b.chain("vector", lambda e, sm=sm: e.tensor_scalar(
            smalls[:, sm + 64:sm + 64 + 1], smalls[:, sm + 64:sm + 64 + 1], 1e-5, None, ALU.add))
        b.wait("scalar", "vv", var_c[1])
        sd_c = b.op("scalar", lambda e, sm=sm: e.activation(
            smalls[:, sm + 80:sm + 80 + 1], smalls[:, sm + 64:sm + 64 + 1], AF.Sqrt), "ss")
        b.wait("vector", "ss", sd_c[1])
        b.chain("vector", lambda e, sm=sm: e.reciprocal(smalls[:, sm + 96:sm + 96 + 1],
                                                     smalls[:, sm + 80:sm + 80 + 1]))
        nmr_c = b.chain("vector", lambda e, sm=sm: e.tensor_tensor(
            smalls[:, sm + 112:sm + 112 + 1], smalls[:, sm + 16:sm + 16 + 1], smalls[:, sm + 96:sm + 96 + 1],
            ALU.mult))
        b.wait("scalar", "vv", nmr_c[1])
        if i >= 2:
            b.wait("scalar", "pe", pe_tp[(i - 2, KC - 1)][1])
        xn_done[i] = b.op("scalar", lambda e, xt=xt, xn=xn, sm=sm: e.activation(
            xn, xt, AF.Identity, scale=smalls[:, sm + 96:sm + 96 + 1],
            bias=smalls[:, sm + 112:sm + 112 + 1]), "ss")
        b.wait("tensor", "ss", xn_done[i][1])
        for k in range(KC):
            pp = k % 2
            if pp in tp_last:
                b.waits("tensor", [tp_last[pp]])
            pe_tp[(i, k)] = b.op("tensor", lambda e, xn=xn, k=k, pp=pp: e.matmul(
                ln_tp[pp][:], xn[:, k * 128:(k + 1) * 128], identb[:],
                start=True, stop=True), "pe")
            eng = "scalar" if (k % 4) < 2 else "vector"
            sig = "ss" if eng == "scalar" else "vv"
            b.wait(eng, "pe", pe_tp[(i, k)][1])
            dst = rbf(o_xnT + k * 4096 + i * 128, 128)
            tp_last[pp] = b.op(eng, lambda e, pp=pp, dst=dst: _cp(e, dst, ln_tp[pp][:]), sig)

    ln_end = {"ss": b.cnt["ss"], "vv": b.cnt["vv"]}
    es_ln.close()

    if upto == "ln":
        dbg_sm = nc.dram_tensor("dbg_sm", [128, 512], F32, kind="ExternalOutput").ap()
        dbg_xn = nc.dram_tensor("dbg_xn", [128, 2048], BF16, kind="ExternalOutput").ap()
        dbg_xt = nc.dram_tensor("dbg_xt", [128, 2048], F32, kind="ExternalOutput").ap()
        sp = None
        for k in range(KC):
            sp = b.dma("sync", xnT_s[k], rbf(o_xnT + k * 4096, 4096),
                       waits=[("ss", ln_end["ss"]), ("vv", ln_end["vv"])] if k == 0 else ())
        sp = b.dma("sync", dbg_sm, smalls[:])
        sp = b.dma("sync", dbg_xn, rbf(o_xn[0], 2048))
        sp = b.dma("sync", dbg_xt, rf32(o_xt[0], 2048))
        b.wait("sync", "qS", sp[1])
        blk = es.enter_context(nc.Block())
        b.emit_all(blk)
        es.close()
        return nc

    es_q = ExitStack()
    qkv_ps = [es_q.enter_context(nc.psum_tensor(f"qkv_ps{i}", [128, 512], F32)) for i in range(8)]

    evac_done = {}
    store_done = {}
    pe_m = [None] * MCQ
    for m in range(MCQ):
        par = m % 2
        wq = rbf(o_wq[par], 2048)
        wld = b.dma("gpsimd", wq, wqkv_h[m].rearrange("p k c -> p (k c)"),
                    waits=[("pe", pe_m[m - 2][1])] if m >= 2 else ())
        b.wait("tensor", "qG", wld[1])
        if m == 0:
            b.wait("tensor", "ss", ln_end["ss"])
            b.wait("tensor", "vv", ln_end["vv"])
        if m >= 1:
            for ti in range(8):
                b.waits("tensor", [evac_done[(m - 1, ti)]])
        for k in range(KC):
            for ti in range(8):
                last = (k == KC - 1 and ti == 7)
                r = b.op("tensor", lambda e, wq=wq, k=k, ti=ti: e.matmul(
                    qkv_ps[ti][:], wq[:, k * 128:(k + 1) * 128],
                    rbf(o_xnT + k * 4096 + ti * 512, 512),
                    start=(k == 0), stop=(k == KC - 1)), "pe" if last else None)
                if last:
                    pe_m[m] = r
        for ti in range(8):
            eng = "scalar" if ti < 4 else "vector"
            sig = "ss" if eng == "scalar" else "vv"
            stg = rbf(o_stage + (ti % 4) * 512, 512)
            b.wait(eng, "pe", pe_m[m][1])
            prev = store_done.get((m, ti - 4)) or store_done.get((m - 1, ti + 4 - 8))
            if ti >= 4:
                prev = store_done.get((m, ti - 4))
            else:
                prev = store_done.get((m - 1, ti + 4))
            if prev:
                b.wait(eng, "qS", prev[1])
            if m < 16 and eng == "scalar":
                cnt = b.op(eng, lambda e, ti=ti, stg=stg: e.activation(
                    stg, qkv_ps[ti][:], AF.Identity, scale=QSCALE), sig)
            elif m < 16:
                cnt = b.op(eng, lambda e, ti=ti, stg=stg: e.tensor_scalar(
                    stg, qkv_ps[ti][:], QSCALE, None, ALU.mult), sig)
            else:
                cnt = b.op(eng, lambda e, ti=ti, stg=stg: _cp(e, stg, qkv_ps[ti][:]), sig)
            evac_done[(m, ti)] = cnt
            store_done[(m, ti)] = b.dma("sync", qkv_s[m, ti], stg, waits=[cnt])
    qkv_end = {"ss": b.cnt["ss"], "vv": b.cnt["vv"]}
    es_q.close()

    if upto == "qkv":
        b.wait("sync", "qS", store_done[(MCQ - 1, 7)][1])
        sp = None
        for k in range(KC):
            sp = b.dma("sync", xnT_s[k], rbf(o_xnT + k * 4096, 4096),
                       waits=[("pe", pe_m[MCQ - 1][1])] if k == 0 else ())
        b.wait("sync", "qS", sp[1])
        blk = es.enter_context(nc.Block())
        b.emit_all(blk)
        es.close()
        return nc

    spill = None
    for k in range(KC):
        spill = b.dma("sync", xnT_s[k], rbf(o_xnT + k * 4096, 4096),
                      waits=[("pe", pe_m[MCQ - 1][1])] if k == 0 else ())

    a_wout, a_attnoT = 0, 32768
    a_q = [65536, 67584]
    a_k = [69632, 71680]
    a_v = 73728
    a_vtok = 75776
    a_PT = [77824, 78336]
    a_sq = 78848
    a_rb = 80896
    a_sst = [81920, 90112]
    a_bb = 86016

    dbg_negc = nc.dram_tensor("dbg_negc", [32, 128, 1], F32, kind="ExternalOutput").ap() \
        if debug else None
    dbg_rows = nc.dram_tensor("dbg_rows", [32, 1, 1536], F32, kind="ExternalOutput").ap() \
        if debug else None
    wout_ld = None
    for k in range(KC):
        wout_ld = b.dma("gpsimd", rbf(a_wout + k * 2048, 2048),
                        wout_h[k * 128:(k + 1) * 128, :],
                        waits=[("qS", spill[1])] if k == 0 else ())

    proj_store = [None] * NT128
    hs_end = {}
    stR = [None, None]
    ptR = [None, None]
    avR = [None, None]
    smR = None
    last_vt_ev = None

    prev_proj_ev = None
    for s in range(SEQ):
        es_a = ExitStack()
        st2 = [es_a.enter_context(nc.psum_tensor(f"st{s}_{i}", [128, 512], F32)) for i in range(2)]
        av2 = [es_a.enter_context(nc.psum_tensor(f"av{s}_{i}", [128, 512], F32)) for i in range(2)]
        l2 = [es_a.enter_context(nc.psum_tensor(f"l{s}_{i}", [128, 512], F32)) for i in range(2)]
        sm2 = [es_a.enter_context(nc.psum_tensor(f"sm{s}_{i}", [128, 512], F32)) for i in range(2)]
        if s == 0:
            b.wait("tensor", "ss", qkv_end["ss"])
            b.wait("tensor", "vv", qkv_end["vv"])
            for eng in ("tensor", "vector", "scalar", "gpsimd"):
                b.wait(eng, "qS", b.cnt["qS"])
                b.wait(eng, "qG", b.cnt["qG"])
        stR = [prev_proj_ev, prev_proj_ev]
        ptR = [None, None]
        avR = [prev_proj_ev, prev_proj_ev]
        smR = prev_proj_ev
        for hidx, h in enumerate([1, 0] + list(range(2, HEADS))):
            hs = s * HEADS + hidx
            par = hs % 2
            gq = [hs_end[hs - 2]["pe"], hs_end[hs - 2]["vv"]] if hs - 2 in hs_end or (hs >= 2) else []
            guard = ([("pe", hs_end[hs - 2]["pe"]), ("vv", hs_end[hs - 2]["vv"])]
                     if hs >= 2 else [])
            vguard = ([("pe", hs_end[hs - 1]["pe"]), ("vv", hs_end[hs - 1]["vv"])]
                      if hs >= 1 else [])
            lds = {}
            for nm, mm_, gg_ in (("q", h, guard), ("k", 16 + h, []), ("v", 32 + h, vguard)):
                base = {"q": a_q[par], "k": a_k[par], "v": a_v}[nm]
                for ti in range(4):
                    lds[(nm, ti)] = b.dma("sync", rbf(base + ti * 512, 512),
                                          qkv_s[mm_, 4 * s + ti],
                                          waits=gg_ if ti == 0 else ())
            b.wait("tensor", "qS", lds[("v", 3)][1])
            vt_ev = {}
            for jc in range(16):
                pp = jc % 2
                if stR[pp]:
                    b.waits("tensor", [stR[pp]])
                if jc >= 2:
                    b.waits("tensor", [vt_ev[jc - 2]])
                vpe = b.op("tensor", lambda e, jc=jc, pp=pp: e.matmul(
                    st2[pp][:, 0:128], rbf(a_v + jc * 128, 128), identb[:],
                    start=True, stop=True), "pe")
                stR[pp] = None
                eng = "scalar" if jc % 2 == 0 else "vector"
                sig = "ss" if eng == "scalar" else "vv"
                b.wait(eng, "pe", vpe[1])
                vt_ev[jc] = b.op(eng, lambda e, jc=jc, pp=pp: _cp(e, 
                    rbf(a_vtok + jc * 128, 128), st2[pp][:, 0:128]), sig)
                stR[pp] = vt_ev[jc]
            last_vt_ev = vt_ev[15]

            b.wait("scalar", "qS", lds[("q", 3)][1])
            sq_c = b.op("scalar", lambda e: e.activation(rbf(a_sq, 2048), rbf(a_q[par], 2048),
                                                         AF.Square), "ss")
            b.wait("tensor", "ss", sq_c[1])
            if smR:
                b.waits("tensor", [smR])
            qn = [None, None]
            for half in range(2):
                b.op("tensor", lambda e, half=half: e.matmul(
                    sm2[half][0:1, :], ones_cb[:], rbf(a_sq + half * 1024, 512),
                    start=True, stop=False))
                qn[half] = b.op("tensor", lambda e, half=half: e.matmul(
                    sm2[half][0:1, :], ones_cb[:], rbf(a_sq + 1024 + half * 512, 512),
                    start=False, stop=True), "pe")
            b.wait("vector", "pe", qn[1][1])
            b.chain("vector", lambda e: e.reduce_max(rows[0:1, 1024:1025], sm2[0][0:1, :], axis=AX.X))
            mq = b.chain("vector", lambda e: e.reduce_max(rows[0:1, 1040:1041], sm2[1][0:1, :],
                                                          axis=AX.X))
            b.wait("scalar", "qS", lds[("k", 3)][1])
            b.wait("scalar", "vv", mq[1])
            sqk = b.op("scalar", lambda e: e.activation(rbf(a_sq, 2048), rbf(a_k[par], 2048),
                                                        AF.Square), "ss")
            b.wait("tensor", "ss", sqk[1])
            b.wait("tensor", "vv", mq[1])
            kn = [None, None]
            for half in range(2):
                b.op("tensor", lambda e, half=half: e.matmul(
                    sm2[half][0:1, :], ones_cb[:], rbf(a_sq + half * 1024, 512),
                    start=True, stop=False))
                kn[half] = b.op("tensor", lambda e, half=half: e.matmul(
                    sm2[half][0:1, :], ones_cb[:], rbf(a_sq + 1024 + half * 512, 512),
                    start=False, stop=True), "pe")
            b.wait("vector", "pe", kn[1][1])
            b.chain("vector", lambda e: e.reduce_max(rows[0:1, 1056:1057], sm2[0][0:1, :], axis=AX.X))
            b.chain("vector", lambda e: e.reduce_max(rows[0:1, 1072:1073], sm2[1][0:1, :], axis=AX.X))
            b.chain("vector", lambda e: e.tensor_tensor(rows[0:1, 1024:1025], rows[0:1, 1024:1025],
                                                        rows[0:1, 1040:1041], ALU.max))
            b.chain("vector", lambda e: e.tensor_tensor(rows[0:1, 1056:1057], rows[0:1, 1056:1057],
                                                        rows[0:1, 1072:1073], ALU.max))
            m2c = b.chain("vector", lambda e: e.tensor_tensor(rows[0:1, 1088:1089],
                                                              rows[0:1, 1024:1025],
                                                              rows[0:1, 1056:1057], ALU.mult))
            b.wait("scalar", "vv", m2c[1])
            b.chain("scalar", lambda e: e.activation(rows[0:1, 1104:1105], rows[0:1, 1088:1089],
                                                     AF.Ln))
            cc = b.chain("scalar", lambda e: e.activation(rows[0:1, 1120:1121], rows[0:1, 1104:1105],
                                                          AF.Exp, scale=0.5))
            b.wait("vector", "ss", cc[1])
            ncr = b.chain("vector", lambda e: e.tensor_scalar(rows[0:1, 1136:1137],
                                                              rows[0:1, 1120:1121], -1.0, None,
                                                              ALU.mult))
            b.wait("tensor", "vv", ncr[1])
            nb = b.op("tensor", lambda e: e.matmul(sm2[0][:, 0:1], ones_rf[:],
                                                   rows[0:1, 1136:1137],
                                                   start=True, stop=True), "pe")
            b.wait("vector", "pe", nb[1])
            negc = b.op("vector", lambda e: _cp(e, smalls[:, 320:321], sm2[0][:, 0:1]), "vv")
            smR = negc

            for ig in range(4):
                igp = ig % 2
                njc = 4 * ig + 4
                exp_d = {}
                for jc in range(njc):
                    tloc = jc - 4 * ig
                    ioff = 128 * tloc if tloc >= 0 else 0
                    stp = jc % 2
                    if stR[stp]:
                        b.waits("tensor", [stR[stp]])
                    smm = b.op("tensor", lambda e, jc=jc, ig=ig, stp=stp, ioff=ioff, par=par:
                               e.matmul(st2[stp][:, ioff:512], rbf(a_k[par] + jc * 128, 128),
                                        rbf(a_q[par] + ig * 512 + ioff, 512 - ioff),
                                        start=True, stop=True), "pe")
                    if tloc >= 0:
                        b.wait("vector", "pe", smm[1])
                        mk = b.op("vector", lambda e, stp=stp, ioff=ioff: e.tensor_tensor(
                            st2[stp][:, ioff:ioff + 128], st2[stp][:, ioff:ioff + 128],
                            maskT[:], ALU.add), "vv")
                        b.wait("scalar", "vv", mk[1])
                    else:
                        b.wait("scalar", "pe", smm[1])
                    zg = None
                    if tloc >= 1:
                        if ptR[stp]:
                            b.waits("gpsimd", [ptR[stp]])
                        zg = b.op("gpsimd", lambda e, stp=stp, ioff=ioff: e.memset(
                            rbf(a_PT[stp], ioff), 0.0), "gg")
                    if ptR[stp]:
                        b.waits("scalar", [ptR[stp]])
                    b.wait("scalar", "vv", negc[1])
                    exp_d[jc] = b.op("scalar", lambda e, stp=stp, ioff=ioff: e.activation(
                        rbf(a_PT[stp] + ioff, 512 - ioff), st2[stp][:, ioff:512], AF.Exp,
                        bias=smalls[:, 320:321]), "ss")
                    stR[stp] = exp_d[jc]
                    b.wait("tensor", "ss", exp_d[jc][1])
                    if zg:
                        b.wait("tensor", "gg", zg[1])
                    if jc == 0:
                        if avR[igp]:
                            b.waits("tensor", [avR[igp]])
                        if hs == 0 and ig == 0:
                            b.wait("tensor", "qS", spill[1])
                    b.waits("tensor", [vt_ev[jc]])
                    b.op("tensor", lambda e, jc=jc, igp=igp, stp=stp: e.matmul(
                        av2[igp][:], rbf(a_vtok + jc * 128, 128), rbf(a_PT[stp], 512),
                        start=(jc == 0), stop=(jc == njc - 1)))
                    lmm = b.op("tensor", lambda e, jc=jc, igp=igp, stp=stp: e.matmul(
                        l2[igp][0:1, :], ones_cb[:], rbf(a_PT[stp], 512),
                        start=(jc == 0), stop=(jc == njc - 1)), "pe")
                    ptR[stp] = lmm
                b.wait("vector", "pe", ptR[(njc - 1) % 2][1])
                b.chain("vector", lambda e, igp=igp: e.tensor_scalar(
                    rows[0:1, 0:512], l2[igp][0:1, :], 1e-35, None, ALU.add))
                rr = b.chain("vector", lambda e: e.reciprocal(rows[0:1, 512:1024],
                                                              rows[0:1, 0:512]))
                if smR:
                    b.waits("tensor", [smR])
                b.wait("tensor", "vv", rr[1])
                rbp = b.op("tensor", lambda e, igp=igp: e.matmul(
                    sm2[igp][:], ones_rf[:], rows[0:1, 512:1024], start=True, stop=True), "pe")
                b.wait("vector", "pe", rbp[1])
                rbc = b.op("vector", lambda e, igp=igp: _cp(e, rf32(a_rb, 512),
                                                                      sm2[igp][:]), "vv")
                ave = b.op("vector", lambda e, igp=igp, h=h, ig=ig: e.tensor_tensor(
                    rbf(a_attnoT + h * 2048 + ig * 512, 512), av2[igp][:], rf32(a_rb, 512),
                    ALU.mult), "vv")
                avR[igp] = ave
                smR = rbc
            hs_end[hs] = {"pe": b.cnt["pe"], "vv": b.cnt["vv"], "ss": b.cnt["ss"]}

        if debug:
            dbg_aot = nc.dram_tensor(f"dbg_aot{s}", [128, 32768], BF16,
                                     kind="ExternalOutput").ap()
            b.dma("sync", dbg_aot, rbf(a_attnoT, 32768),
                  waits=[("vv", hs_end[s * HEADS + HEADS - 1]["vv"])])
        bld = b.dma("sync", rf32(a_bb, 2048)[0:1, :], bout_h,
                    waits=[("vv", hs_end[s * HEADS + HEADS - 1]["vv"])])
        b.wait("tensor", "qS", bld[1])
        b.wait("tensor", "qG", wout_ld[1])
        if smR:
            b.waits("tensor", [smR])
        for ig in range(2):
            if avR[ig]:
                b.waits("tensor", [avR[ig]])
        bbp = []
        for n in range(4):
            bbp.append(b.op("tensor", lambda e, n=n: e.matmul(
                sm2[n % 2][:], ones_rf[:], rf32(a_bb, 2048)[0:1, n * 512:(n + 1) * 512],
                start=True, stop=True), "pe"))
        bbc = None
        for n in range(4):
            b.wait("vector", "pe", bbp[n][1])
            bbc = b.op("vector", lambda e, n=n: _cp(e, 
                rf32(a_bb + 1024 * n, 512), sm2[n % 2][:]), "vv")
        es_a.close()
        es_p = ExitStack()
        pj = [es_p.enter_context(nc.psum_tensor(f"pj{s}_{i}", [128, 2048], F32)) for i in range(2)]
        pj_ev = [None, None]
        for t in range(16):
            tpar = t % 2
            gidx = s * 16 + t
            if t == 0:
                b.wait("tensor", "vv", bbc[1])
            if pj_ev[tpar]:
                b.waits("tensor", [pj_ev[tpar]])
            pj_pe = None
            for k in range(KC):
                for n in range(4):
                    last = (k == KC - 1 and n == 3)
                    r = b.op("tensor", lambda e, k=k, n=n, t=t, tpar=tpar: e.matmul(
                        pj[tpar][:, n * 512:(n + 1) * 512],
                        rbf(a_attnoT + k * 2048 + t * 128, 128),
                        rbf(a_wout + k * 2048 + n * 512, 512),
                        start=(k == 0), stop=(k == KC - 1)), "pe" if last else None)
                    if last:
                        pj_pe = r
            b.wait("vector", "pe", pj_pe[1])
            if gidx >= 2:
                b.wait("vector", "qS", proj_store[gidx - 2][1])
            pj_ev[tpar] = b.op("vector", lambda e, tpar=tpar: e.tensor_tensor(
                rf32(a_sst[tpar], 2048), pj[tpar][:], rf32(a_bb, 2048), ALU.add), "vv")
            proj_store[gidx] = b.dma("sync", s_s[gidx], rf32(a_sst[tpar], 2048),
                                     waits=[pj_ev[tpar]])
        if debug:
            dbg_bb = nc.dram_tensor(f"dbg_bb{s}", [128, 2048], F32,
                                    kind="ExternalOutput").ap()
            b.dma("sync", dbg_bb, rf32(a_bb, 2048), waits=[pj_ev[1]])
        prev_proj_ev = pj_ev[1]
        es_p.close()

    if upto == "proj":
        b.wait("sync", "qS", proj_store[NT128 - 1][1])
        blk = es.enter_context(nc.Block())
        b.emit_all(blk)
        es.close()
        return nc

    q_chunk = [0, 8192]
    q_aa = 16384
    q_qq = [24576, 26624]
    q_part = 28672

    es_qz = ExitStack()
    qt_ps = es_qz.enter_context(nc.psum_tensor("qt_ps", [128, 512], F32))
    lp_vv = b.cnt["vv"]
    lp_st = proj_store[NT128 - 1]
    last_q_store = None
    for wi in range(3):
        wh = (w1_h, w2_h, w3_h)[wi]
        qs_ = qw_s[wi]
        flat = wh.rearrange("a b c d -> (a b) (c d)") if wi < 2 else wh
        if wi == 0:
            b.waits("gpsimd", [("vv", lp_vv), ("qS", lp_st[1])])
        else:
            b.wait("gpsimd", "pe", gs[1])
        zq = b.op("gpsimd", lambda e: e.memset(rf32(q_part, 720), 0.0), "gg")
        b.wait("vector", "gg", zq[1])
        red = {}
        for ci in range(MCF):
            par = ci % 2
            if wi == 0 and ci < 2:
                w = [("vv", lp_vv), ("qS", lp_st[1])]
            elif ci < 2:
                w = [("vv", qq_d[MCF - 2 + ci][1])]
            else:
                w = [("vv", red[ci - 2][1])]
            ld = b.dma("sync", rf32(q_chunk[par], 2048), flat[ci * 128:(ci + 1) * 128, :],
                       waits=w)
            b.wait("vector", "qS", ld[1])
            red[ci] = b.op("vector", lambda e, ci=ci, par=par: e.tensor_reduce(
                rf32(q_part, 720)[:, 16 * ci:16 * ci + 1], rf32(q_chunk[par], 2048), AX.X,
                ALU.add, apply_absolute_value=True), "vv")
        b.wait("vector", "vv", red[MCF - 1][1])
        p1 = b.chain("vector", lambda e: e.reduce_sum(rf32(q_part, 720)[:, 704:705],
                                                      rf32(q_part, 720)[:, 0:704], axis=AX.X))
        b.wait("tensor", "vv", p1[1])
        gs = b.op("tensor", lambda e: e.matmul(qt_ps[0:1, 0:1], rf32(q_part, 720)[:, 704:705],
                                               ones_cf[:], start=True, stop=True), "pe")
        b.wait("vector", "pe", gs[1])
        b.chain("vector", lambda e: e.tensor_scalar(rows[0:1, 1168:1169], qt_ps[0:1, 0:1],
                                                    0.5 / N_W, None, ALU.mult))
        thr = b.chain("vector", lambda e: e.tensor_scalar(rows[0:1, 1168:1169],
                                                          rows[0:1, 1168:1169], 0.5e-8, None,
                                                          ALU.add))
        b.wait("tensor", "vv", thr[1])
        thb = b.op("tensor", lambda e: e.matmul(qt_ps[:, 0:1], ones_rf[:],
                                                rows[0:1, 1168:1169], start=True, stop=True), "pe")
        b.wait("vector", "pe", thb[1])
        thbc = b.op("vector", lambda e: _cp(e, smalls[:, 336:337], qt_ps[:, 0:1]), "vv")
        qq_d = {}
        qst = {}
        for ci in range(MCF):
            par = ci % 2
            w = [("vv", thbc[1])] if ci < 2 else [("vv", qq_d[ci - 2][1])]
            ld = b.dma("gpsimd", rf32(q_chunk[par], 2048), flat[ci * 128:(ci + 1) * 128, :],
                       waits=w)
            b.wait("vector", "qG", ld[1])
            if ci >= 2:
                b.wait("vector", "qS", qst[ci - 2][1])
            b.op("vector", lambda e, par=par: e.tensor_scalar(
                rf32(q_aa, 2048), rf32(q_chunk[par], 2048), smalls[:, 336:337], 0.0,
                ALU.add, ALU.is_lt))
            qq_d[ci] = b.op("vector", lambda e, par=par: e.scalar_tensor_tensor(
                rbf(q_qq[par], 2048), rf32(q_chunk[par], 2048), smalls[:, 336:337],
                rf32(q_aa, 2048), ALU.is_gt, ALU.subtract), "vv")
            qst[ci] = b.dma("sync", qs_[ci], rbf(q_qq[par], 2048), waits=[qq_d[ci]])
        last_q_store = qst[MCF - 1]
    es_qz.close()

    f_gsb = 0
    f_xnT = 32768
    f_hid = 49152
    f_qw1 = [71680, 73728]
    f_qw2 = [75776, 77824]
    f_qw3 = [79872, 80896]
    f_sig = [83968, 84992]
    f_st = 86016
    f_xt = 90112
    f_sgm = 94208

    es_f = ExitStack()
    ps4 = [es_f.enter_context(nc.psum_tensor(f"fps{i}", [128, 1024], F32)) for i in range(4)]

    dbg_hid = nc.dram_tensor("dbg_hid", [MCF, 128, 1024], BF16, kind=dk).ap() if debug else None
    dbg_gsb = nc.dram_tensor("dbg_gsb", [8, 128, 2048], F32, kind=dk).ap() if debug else None
    ffn3_pe = None
    prev_gev = None
    out_store = [None] * NT128
    hid_d = {}
    x12_d = {}
    g_ev = {}

    for tt in range(4):
        s, tl = divmod(tt, 2)
        xn_lds = None
        for k in range(KC):
            w = []
            if k == 0:
                w = [("pe", (x12_d[(tt - 1, 1, 21)][1] if tt > 0 else 0))]
                if tt == 0:
                    w.append(("qS", last_q_store[1]))
            xn_lds = b.dma("sync", rbf(f_xnT + k * 1024, 1024),
                           xnT_s[k][:, tt * 1024:(tt + 1) * 1024], waits=w)
        for half in range(2):
            mbase = 22 * half
            for ml in range(22):
                m = mbase + ml
                par = ml % 2
                w1w = [("pe", x12_d[(tt, half, ml - 2)][1])] if ml >= 2 else ()
                l1 = b.dma("gpsimd", rbf(f_qw1[par], 2048), qw_s[0][m], waits=w1w)
                l2_ = b.dma("gpsimd", rbf(f_qw2[par], 2048), qw_s[1][m])
                b.wait("tensor", "qG", l2_[1])
                if ml == 0 and half == 0:
                    b.wait("tensor", "qS", xn_lds[1])
                if ml < 2:
                    if prev_gev:
                        b.waits("tensor", [prev_gev])
                elif ml >= 2:
                    b.wait("tensor", "vv", hid_d[(tt, half, ml - 2)][1])
                for k in range(KC):
                    for n in range(2):
                        b.op("tensor", lambda e, par=par, k=k, n=n: e.matmul(
                            ps4[par][:, n * 512:(n + 1) * 512], rbf(f_qw1[par] + k * 128, 128),
                            rbf(f_xnT + k * 1024 + n * 512, 512),
                            start=(k == 0), stop=(k == KC - 1)))
                for k in range(KC):
                    for n in range(2):
                        last = (k == KC - 1 and n == 1)
                        r = b.op("tensor", lambda e, par=par, k=k, n=n: e.matmul(
                            ps4[2 + par][:, n * 512:(n + 1) * 512], rbf(f_qw2[par] + k * 128, 128),
                            rbf(f_xnT + k * 1024 + n * 512, 512),
                            start=(k == 0), stop=(k == KC - 1)), "pe" if last else None)
                        if last:
                            x12_d[(tt, half, ml)] = r
                b.wait("scalar", "pe", x12_d[(tt, half, ml)][1])
                if ml >= 2:
                    b.wait("scalar", "vv", hid_d[(tt, half, ml - 2)][1])
                sg = b.op("scalar", lambda e, par=par: e.activation(
                    rbf(f_sig[par], 1024), ps4[par][:], AF.Sigmoid), "ss")
                b.wait("vector", "ss", sg[1])
                hid_d[(tt, half, ml)] = b.op("vector", lambda e, par=par, ml=ml: e.tensor_tensor(
                    rbf(f_hid + ml * 1024, 1024), ps4[2 + par][:], rbf(f_sig[par], 1024),
                    ALU.mult), "vv")
                if debug and tt == 0:
                    b.dma("sync", dbg_hid[mbase + ml], rbf(f_hid + ml * 1024, 1024),
                          waits=[hid_d[(tt, half, ml)]])
            b.wait("tensor", "vv", hid_d[(tt, half, 21)][1])
            for quad in range(2):
                for nh in range(2):
                    ml_pe = {}
                    for ml in range(22):
                        m = mbase + ml
                        par = ml % 2
                        if ml >= 2:
                            w = [("pe", ml_pe[ml - 2][1])]
                        elif ffn3_pe is not None:
                            w = [("pe", ffn3_pe[1])]
                        else:
                            w = []
                        l3 = b.dma("gpsimd", rbf(f_qw3[par], 1024),
                                   qw_s[2][m][:, nh * 1024:(nh + 1) * 1024], waits=w)
                        b.wait("tensor", "qG", l3[1])
                        if ml == 0 and prev_gev:
                            b.waits("tensor", [prev_gev])
                        for t4 in range(4):
                            t128 = quad * 4 + t4
                            for nn in range(2):
                                lastml = (t4 == 3 and nn == 1)
                                r = b.op("tensor", lambda e, ml=ml, par=par, t4=t4, t128=t128,
                                         nn=nn: e.matmul(
                                             ps4[t4][:, nn * 512:(nn + 1) * 512],
                                             rbf(f_hid + ml * 1024 + t128 * 128, 128),
                                             rbf(f_qw3[par] + nn * 512, 512),
                                             start=(ml == 0), stop=(ml == 21)),
                                         "pe" if lastml else None)
                                if lastml:
                                    ml_pe[ml] = r
                    ffn3_pe = ml_pe[21]
                    for t4 in range(4):
                        t128 = quad * 4 + t4
                        dst = rf32(f_gsb + t128 * 4096 + nh * 2048, 1024)
                        b.wait("vector", "pe", ffn3_pe[1])
                        if half == 0 and tt > 0:
                            b.wait("vector", "qS", out_store[(tt - 1) * 8 + t128][1])
                        if half == 0:
                            ev = b.op("vector", lambda e, t4=t4, dst=dst: _cp(e, 
                                dst, ps4[t4][:]), "vv")
                        else:
                            ev = b.op("vector", lambda e, t4=t4, dst=dst: e.tensor_tensor(
                                dst, ps4[t4][:], dst, ALU.add), "vv")
                            g_ev[(tt, t128, nh)] = ev
                    prev_gev = ev
        dbg_gd = None
        if debug and tt == 0:
            for t128 in range(8):
                dbg_gd = b.dma("sync", dbg_gsb[t128], rf32(f_gsb + t128 * 4096, 2048),
                               waits=[g_ev[(tt, t128, 0)], g_ev[(tt, t128, 1)]])
        sgm_rd = None
        for t128 in range(8):
            gt = tt * 8 + t128
            trow = tl * 8 + t128
            gsb = rf32(f_gsb + t128 * 4096, 2048)
            sld = b.dma("sync", rf32(f_st, 2048), s_s[s * 16 + trow],
                        waits=[g_ev[(tt, t128, 1)]] if True else ())
            xld = b.dma("sync", rf32(f_xt, 2048), x_h[s, trow * 128:(trow + 1) * 128, :])
            b.waits("scalar", [g_ev[(tt, t128, 0)], g_ev[(tt, t128, 1)]])
            if sgm_rd:
                b.wait("scalar", "vv", sgm_rd[1])
            if dbg_gd and t128 == 0:
                b.wait("scalar", "qS", dbg_gd[1])
            sgc = b.op("scalar", lambda e, gsb=gsb: e.activation(
                rbf(f_sgm, 2048), gsb, AF.Sigmoid), "ss")
            b.wait("vector", "ss", sgc[1])
            sgm_rd = b.chain("vector", lambda e, gsb=gsb: e.tensor_tensor(
                gsb, gsb, rbf(f_sgm, 2048), ALU.mult))
            b.wait("vector", "qS", sld[1])
            b.chain("vector", lambda e, gsb=gsb: e.tensor_tensor(gsb, gsb, rf32(f_st, 2048),
                                                                 ALU.mult))
            b.wait("vector", "qS", xld[1])
            fin = b.chain("vector", lambda e, gsb=gsb: e.tensor_tensor(
                gsb, gsb, rf32(f_xt, 2048), ALU.add))
            out_store[gt] = b.dma("sync", out_h[s, trow * 128:(trow + 1) * 128, :], gsb,
                                  waits=[fin])
    es_f.close()

    b.wait("sync", "qS", out_store[NT128 - 1][1])
    b.wait("gpsimd", "gg", g_ones[1])

    blk = es.enter_context(nc.Block())
    b.emit_all(blk)
    es.close()
    return nc


def _prep_weights(wqkv, wout, bout, w1, w2, w3):
    wqkv = np.ascontiguousarray(
        np.asarray(wqkv, dtype=np.float32).reshape(KC, 128, MCQ, 128).transpose(2, 1, 0, 3))
    wout = np.ascontiguousarray(np.asarray(wout, dtype=np.float32))
    bout = np.ascontiguousarray(np.asarray(bout, dtype=np.float32).reshape(1, D))
    w1 = np.ascontiguousarray(
        np.asarray(w1, dtype=np.float32).T.reshape(KC, 128, MCF, 128).transpose(2, 1, 0, 3))
    w2 = np.ascontiguousarray(
        np.asarray(w2, dtype=np.float32).T.reshape(KC, 128, MCF, 128).transpose(2, 1, 0, 3))
    w3 = np.ascontiguousarray(np.asarray(w3, dtype=np.float32).T)
    return wqkv, wout, bout, w1, w2, w3


def kernel(L0_attn_x, L0_attn_w_qkv, L0_attn_w_out, L0_attn_b_out,
           L0_ffn_w1, L0_ffn_w2, L0_ffn_w3):
    x = np.ascontiguousarray(np.asarray(L0_attn_x, dtype=np.float32))
    wqkv, wout, bout, w1, w2, w3 = _prep_weights(
        L0_attn_w_qkv, L0_attn_w_out, L0_attn_b_out, L0_ffn_w1, L0_ffn_w2, L0_ffn_w3)
    nc = build_program()
    in_maps = [{"x": np.ascontiguousarray(x[2 * c:2 * c + 2]),
                "wqkv": wqkv, "wout": wout, "bout": bout,
                "w1": w1, "w2": w2, "w3": w3} for c in range(8)]
    res = run_bass_kernel_spmd(nc, in_maps, core_ids=list(range(8)))
    return np.concatenate([r["out"] for r in res.results], axis=0)


# version 03
# speedup vs baseline: 1.0895x; 1.0895x over previous
import math
import numpy as np
from contextlib import ExitStack

import concourse.bass as bass
import concourse.mybir as mybir
from concourse.bass_utils import run_bass_kernel_spmd

F32 = mybir.dt.float32
BF16 = mybir.dt.bfloat16
I32 = mybir.dt.int32
AF = mybir.ActivationFunctionType
ALU = mybir.AluOpType
AX = mybir.AxisListType

SEQ, T, D, KC, HEADS, DH = 2, 2048, 2048, 16, 16, 128
MCQ, HF, MCF = 48, 5632, 44
TOK, NT128 = SEQ * T, 32
QSCALE = float(1.0 / math.sqrt(DH))
NEG = -1e30
N_W = float(HF * D)

ENGS = ("sync", "gpsimd", "vector", "scalar", "tensor")


def _cp(e, dst, src):
    f = getattr(e, "tensor_copy", None) or e.copy
    return f(dst, src)


class Bld:
    def __init__(self, nc, es):
        self.nc = nc
        self.ev = {e: [] for e in ENGS}
        self.cnt = {}
        self.h = {}
        for name in ("qS", "qG", "qA", "pe", "vv", "ss", "gg"):
            self.h[name] = es.enter_context(nc.semaphore(name))
            self.cnt[name] = 0

    def on(self, eng, fn):
        self.ev[eng].append(fn)

    def wait(self, eng, sem, val):
        if not val:
            return
        h = self.h[sem]
        self.on(eng, lambda e, h=h, v=val: e.wait_ge(h, v))

    def waits(self, eng, pairs):
        for sem, val in pairs:
            self.wait(eng, sem, val)

    def op(self, eng, emit, sig=None, n=1):
        if sig is None:
            self.on(eng, emit)
            return None
        self.cnt[sig] += n
        v = self.cnt[sig]
        h = self.h[sig]
        self.on(eng, lambda e, emit=emit, h=h, n=n: emit(e).then_inc(h, n))
        return (sig, v)

    def chain(self, eng, emit):
        sig = {"vector": "vv", "scalar": "ss", "gpsimd": "gg", "tensor": "pe"}[eng]
        r = self.op(eng, emit, sig)
        self.wait(eng, sig, r[1])
        return r

    def dma(self, q, out, in_, waits=()):
        self.waits(q, waits)
        sig = "qS" if q == "sync" else "qG"
        return self.op(q, lambda e, o=out, i=in_: e.dma_start(out=o, in_=i), sig, 16)

    def emit_all(self, blk):
        b = self

        @blk.sync
        def _(e):
            for f in b.ev["sync"]:
                f(e)

        @blk.gpsimd
        def _(e):
            for f in b.ev["gpsimd"]:
                f(e)

        @blk.vector
        def _(e):
            for f in b.ev["vector"]:
                f(e)

        @blk.scalar
        def _(e):
            for f in b.ev["scalar"]:
                f(e)

        @blk.tensor
        def _(e):
            for f in b.ev["tensor"]:
                f(e)


def build_program(debug=False, upto="all"):
    nc = bass.Bass("TRN2", target_bir_lowering=False)
    es = ExitStack()
    dk = "ExternalOutput" if debug else "Internal"

    x_h = nc.dram_tensor("x", [SEQ, T, D], F32, kind="ExternalInput").ap()
    wqkv_h = nc.dram_tensor("wqkv", [MCQ, 128, KC, 128], F32, kind="ExternalInput").ap()
    wout_h = nc.dram_tensor("wout", [D, D], F32, kind="ExternalInput").ap()
    bout_h = nc.dram_tensor("bout", [1, D], F32, kind="ExternalInput").ap()
    w1_h = nc.dram_tensor("w1", [MCF, 128, KC, 128], F32, kind="ExternalInput").ap()
    w2_h = nc.dram_tensor("w2", [MCF, 128, KC, 128], F32, kind="ExternalInput").ap()
    w3_h = nc.dram_tensor("w3", [HF, D], F32, kind="ExternalInput").ap()
    out_h = nc.dram_tensor("out", [SEQ, T, D], F32, kind="ExternalOutput").ap()

    xnT_s = nc.dram_tensor("xnT_s", [KC, 128, TOK], BF16, kind=dk).ap()
    qkv_s = nc.dram_tensor("qkv_s", [MCQ, 8, 128, 512], BF16, kind=dk).ap()
    s_s = nc.dram_tensor("s_s", [NT128, 128, D], F32, kind=dk).ap()
    qw_s = [nc.dram_tensor(f"qw{i}_s", [MCF, 128, D], BF16, kind=dk).ap() for i in (1, 2, 3)]

    identb = es.enter_context(nc.sbuf_tensor("identb", [128, 128], BF16))
    maskT = es.enter_context(nc.sbuf_tensor("maskT", [128, 128], F32))
    ones_cb = es.enter_context(nc.sbuf_tensor("ones_cb", [128, 1], BF16))
    ones_cf = es.enter_context(nc.sbuf_tensor("ones_cf", [128, 1], F32))
    ones_rf = es.enter_context(nc.sbuf_tensor("ones_rf", [1, 128], F32))
    smalls = es.enter_context(nc.sbuf_tensor("smalls", [128, 512], F32))
    rows = es.enter_context(nc.sbuf_tensor("rows", [1, 1536], F32))

    AR = 98304
    arena = es.enter_context(nc.sbuf_tensor("arena", [128, AR], BF16))

    def rbf(off, n):
        return arena[:, off:off + n]

    def rf32(off, nf32):
        return arena[:, off:off + 2 * nf32].bitcast(F32)

    b = Bld(nc, es)

    iotp = rbf(90112, 256).bitcast(I32)
    iotf = rbf(90624, 256).bitcast(I32)
    b.op("gpsimd", lambda e: e.iota(iotf, pattern=[[1, 128]], base=0, channel_multiplier=0))
    g_iot = b.op("gpsimd", lambda e: e.iota(iotp, pattern=[[0, 128]], base=0,
                                            channel_multiplier=1), "gg")
    b.op("gpsimd", lambda e: e.memset(ones_cb[:], 1.0))
    b.op("gpsimd", lambda e: e.memset(ones_cf[:], 1.0))
    g_ones = b.op("gpsimd", lambda e: e.memset(ones_rf[:], 1.0), "gg")
    b.wait("vector", "gg", g_iot[1])
    b.op("vector", lambda e: e.tensor_tensor(identb[:], iotf, iotp, ALU.is_equal))
    b.op("vector", lambda e: e.tensor_tensor(maskT[:], iotp, iotf, ALU.is_gt))
    v_ini = b.op("vector", lambda e: e.tensor_scalar(maskT[:], maskT[:], NEG, None, ALU.mult), "vv")

    o_xnT = 0
    o_xt = [65536, 65536 + 8192]
    o_scrap = 65536 + 16384
    o_xn = [o_scrap + 4096, o_scrap + 6144]
    o_wq = [o_scrap + 8192, o_scrap + 10240]
    o_stage = o_scrap + 12288

    es_ln = ExitStack()
    ln_tp = [es_ln.enter_context(nc.psum_tensor(f"ln_tp{i}", [128, 128], F32)) for i in range(2)]

    xn_done = [None] * NT128
    tp_last = {}
    pe_tp = {}
    for i in range(NT128):
        s, t = divmod(i, 16)
        par = i % 2
        xt = rf32(o_xt[par], 2048)
        xn = rbf(o_xn[par], 2048)
        sm = par * 160
        xld = b.dma("sync", xt, x_h[s, t * 128:(t + 1) * 128, :],
                    waits=[("ss", xn_done[i - 2][1])] if i >= 2 else ())
        b.wait("vector", "qS", xld[1])
        if i >= 2:
            b.wait("vector", "ss", xn_done[i - 2][1])
        b.chain("vector", lambda e, xt=xt, sm=sm: e.reduce_sum(smalls[:, sm:sm + 1], xt, axis=AX.X))
        b.wait("scalar", "qS", xld[1])
        if i == 0:
            b.wait("scalar", "vv", v_ini[1])
        sq = b.op("scalar", lambda e, xt=xt, sm=sm: e.activation(
            rf32(o_scrap, 2048), xt, AF.Square, accum_out=smalls[:, sm + 128:sm + 128 + 1]), "ss")
        b.wait("vector", "ss", sq[1])
        b.chain("vector", lambda e, sm=sm: e.tensor_scalar(
            smalls[:, sm + 16:sm + 16 + 1], smalls[:, sm:sm + 1], -1.0 / D, None, ALU.mult))
        b.chain("vector", lambda e, sm=sm: e.tensor_scalar(
            smalls[:, sm + 32:sm + 32 + 1], smalls[:, sm + 128:sm + 128 + 1], 1.0 / D, None, ALU.mult))
        b.chain("vector", lambda e, sm=sm: e.tensor_tensor(
            smalls[:, sm + 48:sm + 48 + 1], smalls[:, sm + 16:sm + 16 + 1], smalls[:, sm + 16:sm + 16 + 1], ALU.mult))
        b.chain("vector", lambda e, sm=sm: e.tensor_tensor(
            smalls[:, sm + 64:sm + 64 + 1], smalls[:, sm + 32:sm + 32 + 1], smalls[:, sm + 48:sm + 48 + 1],
            ALU.subtract))
        var_c = b.chain("vector", lambda e, sm=sm: e.tensor_scalar(
            smalls[:, sm + 64:sm + 64 + 1], smalls[:, sm + 64:sm + 64 + 1], 1e-5, None, ALU.add))
        b.wait("scalar", "vv", var_c[1])
        sd_c = b.op("scalar", lambda e, sm=sm: e.activation(
            smalls[:, sm + 80:sm + 80 + 1], smalls[:, sm + 64:sm + 64 + 1], AF.Sqrt), "ss")
        b.wait("vector", "ss", sd_c[1])
        b.chain("vector", lambda e, sm=sm: e.reciprocal(smalls[:, sm + 96:sm + 96 + 1],
                                                     smalls[:, sm + 80:sm + 80 + 1]))
        nmr_c = b.chain("vector", lambda e, sm=sm: e.tensor_tensor(
            smalls[:, sm + 112:sm + 112 + 1], smalls[:, sm + 16:sm + 16 + 1], smalls[:, sm + 96:sm + 96 + 1],
            ALU.mult))
        b.wait("scalar", "vv", nmr_c[1])
        if i >= 2:
            b.wait("scalar", "pe", pe_tp[(i - 2, KC - 1)][1])
        xn_done[i] = b.op("scalar", lambda e, xt=xt, xn=xn, sm=sm: e.activation(
            xn, xt, AF.Identity, scale=smalls[:, sm + 96:sm + 96 + 1],
            bias=smalls[:, sm + 112:sm + 112 + 1]), "ss")
        b.wait("tensor", "ss", xn_done[i][1])
        for k in range(KC):
            pp = k % 2
            if pp in tp_last:
                b.waits("tensor", [tp_last[pp]])
            pe_tp[(i, k)] = b.op("tensor", lambda e, xn=xn, k=k, pp=pp: e.matmul(
                ln_tp[pp][:], xn[:, k * 128:(k + 1) * 128], identb[:],
                start=True, stop=True), "pe")
            eng = "scalar" if (k % 4) < 2 else "vector"
            sig = "ss" if eng == "scalar" else "vv"
            b.wait(eng, "pe", pe_tp[(i, k)][1])
            dst = rbf(o_xnT + k * 4096 + i * 128, 128)
            tp_last[pp] = b.op(eng, lambda e, pp=pp, dst=dst: _cp(e, dst, ln_tp[pp][:]), sig)

    ln_end = {"ss": b.cnt["ss"], "vv": b.cnt["vv"]}
    es_ln.close()

    if upto == "ln":
        dbg_sm = nc.dram_tensor("dbg_sm", [128, 512], F32, kind="ExternalOutput").ap()
        dbg_xn = nc.dram_tensor("dbg_xn", [128, 2048], BF16, kind="ExternalOutput").ap()
        dbg_xt = nc.dram_tensor("dbg_xt", [128, 2048], F32, kind="ExternalOutput").ap()
        sp = None
        for k in range(KC):
            sp = b.dma("sync", xnT_s[k], rbf(o_xnT + k * 4096, 4096),
                       waits=[("ss", ln_end["ss"]), ("vv", ln_end["vv"])] if k == 0 else ())
        sp = b.dma("sync", dbg_sm, smalls[:])
        sp = b.dma("sync", dbg_xn, rbf(o_xn[0], 2048))
        sp = b.dma("sync", dbg_xt, rf32(o_xt[0], 2048))
        b.wait("sync", "qS", sp[1])
        blk = es.enter_context(nc.Block())
        b.emit_all(blk)
        es.close()
        return nc

    es_q = ExitStack()
    qkv_ps = [es_q.enter_context(nc.psum_tensor(f"qkv_ps{i}", [128, 512], F32)) for i in range(8)]

    evac_done = {}
    store_done = {}
    pe_m = [None] * MCQ
    for m in range(MCQ):
        par = m % 2
        wq = rbf(o_wq[par], 2048)
        wld = b.dma("gpsimd", wq, wqkv_h[m].rearrange("p k c -> p (k c)"),
                    waits=[("pe", pe_m[m - 2][1])] if m >= 2 else ())
        b.wait("tensor", "qG", wld[1])
        if m == 0:
            b.wait("tensor", "ss", ln_end["ss"])
            b.wait("tensor", "vv", ln_end["vv"])
        if m >= 1:
            for ti in range(8):
                b.waits("tensor", [evac_done[(m - 1, ti)]])
        for k in range(KC):
            for ti in range(8):
                last = (k == KC - 1 and ti == 7)
                r = b.op("tensor", lambda e, wq=wq, k=k, ti=ti: e.matmul(
                    qkv_ps[ti][:], wq[:, k * 128:(k + 1) * 128],
                    rbf(o_xnT + k * 4096 + ti * 512, 512),
                    start=(k == 0), stop=(k == KC - 1)), "pe" if last else None)
                if last:
                    pe_m[m] = r
        for ti in range(8):
            eng = "scalar" if ti < 4 else "vector"
            sig = "ss" if eng == "scalar" else "vv"
            stg = rbf(o_stage + (ti % 4) * 512, 512)
            b.wait(eng, "pe", pe_m[m][1])
            prev = store_done.get((m, ti - 4)) or store_done.get((m - 1, ti + 4 - 8))
            if ti >= 4:
                prev = store_done.get((m, ti - 4))
            else:
                prev = store_done.get((m - 1, ti + 4))
            if prev:
                b.wait(eng, "qS", prev[1])
            if m < 16 and eng == "scalar":
                cnt = b.op(eng, lambda e, ti=ti, stg=stg: e.activation(
                    stg, qkv_ps[ti][:], AF.Identity, scale=QSCALE), sig)
            elif m < 16:
                cnt = b.op(eng, lambda e, ti=ti, stg=stg: e.tensor_scalar(
                    stg, qkv_ps[ti][:], QSCALE, None, ALU.mult), sig)
            else:
                cnt = b.op(eng, lambda e, ti=ti, stg=stg: _cp(e, stg, qkv_ps[ti][:]), sig)
            evac_done[(m, ti)] = cnt
            store_done[(m, ti)] = b.dma("sync", qkv_s[m, ti], stg, waits=[cnt])
    qkv_end = {"ss": b.cnt["ss"], "vv": b.cnt["vv"]}
    es_q.close()

    if upto == "qkv":
        b.wait("sync", "qS", store_done[(MCQ - 1, 7)][1])
        sp = None
        for k in range(KC):
            sp = b.dma("sync", xnT_s[k], rbf(o_xnT + k * 4096, 4096),
                       waits=[("pe", pe_m[MCQ - 1][1])] if k == 0 else ())
        b.wait("sync", "qS", sp[1])
        blk = es.enter_context(nc.Block())
        b.emit_all(blk)
        es.close()
        return nc

    spill = None
    for k in range(KC):
        spill = b.dma("sync", xnT_s[k], rbf(o_xnT + k * 4096, 4096),
                      waits=[("pe", pe_m[MCQ - 1][1])] if k == 0 else ())

    a_wout, a_attnoT = 0, 32768
    a_q = [65536, 67584]
    a_k = [69632, 71680]
    a_v = 73728
    a_vtok = 75776
    a_PT = [77824, 78336]
    a_sq = 78848
    a_rb = 80896
    a_sst = [81920, 90112]
    a_bb = 86016

    dbg_negc = nc.dram_tensor("dbg_negc", [32, 128, 1], F32, kind="ExternalOutput").ap() \
        if debug else None
    dbg_rows = nc.dram_tensor("dbg_rows", [32, 1, 1536], F32, kind="ExternalOutput").ap() \
        if debug else None
    wout_ld = None
    for k in range(KC):
        wout_ld = b.dma("gpsimd", rbf(a_wout + k * 2048, 2048),
                        wout_h[k * 128:(k + 1) * 128, :],
                        waits=[("qS", spill[1])] if k == 0 else ())

    proj_store = [None] * NT128
    hs_end = {}
    stR = [None, None]
    ptR = [None, None]
    avR = [None, None]
    smR = None
    last_vt_ev = None

    prev_proj_ev = None
    for s in range(SEQ):
        es_a = ExitStack()
        st2 = [es_a.enter_context(nc.psum_tensor(f"st{s}_{i}", [128, 512], F32)) for i in range(2)]
        av2 = [es_a.enter_context(nc.psum_tensor(f"av{s}_{i}", [128, 512], F32)) for i in range(2)]
        l2 = [es_a.enter_context(nc.psum_tensor(f"l{s}_{i}", [128, 512], F32)) for i in range(2)]
        sm2 = [es_a.enter_context(nc.psum_tensor(f"sm{s}_{i}", [128, 512], F32)) for i in range(2)]
        if s == 0:
            b.wait("tensor", "ss", qkv_end["ss"])
            b.wait("tensor", "vv", qkv_end["vv"])
            for eng in ("tensor", "vector", "scalar", "gpsimd"):
                b.wait(eng, "qS", b.cnt["qS"])
                b.wait(eng, "qG", b.cnt["qG"])
        stR = [prev_proj_ev, prev_proj_ev]
        ptR = [None, None]
        avR = [prev_proj_ev, prev_proj_ev]
        smR = prev_proj_ev
        for hidx, h in enumerate([1, 0] + list(range(2, HEADS))):
            hs = s * HEADS + hidx
            par = hs % 2
            gq = [hs_end[hs - 2]["pe"], hs_end[hs - 2]["vv"]] if hs - 2 in hs_end or (hs >= 2) else []
            guard = ([("pe", hs_end[hs - 2]["pe"]), ("vv", hs_end[hs - 2]["vv"])]
                     if hs >= 2 else [])
            vguard = ([("pe", hs_end[hs - 1]["pe"]), ("vv", hs_end[hs - 1]["vv"])]
                      if hs >= 1 else [])
            lds = {}
            for nm, mm_, gg_ in (("q", h, guard), ("k", 16 + h, []), ("v", 32 + h, vguard)):
                base = {"q": a_q[par], "k": a_k[par], "v": a_v}[nm]
                for ti in range(4):
                    lds[(nm, ti)] = b.dma("sync", rbf(base + ti * 512, 512),
                                          qkv_s[mm_, 4 * s + ti],
                                          waits=gg_ if ti == 0 else ())
            b.wait("tensor", "qS", lds[("v", 3)][1])
            vt_ev = {}
            for jc in range(16):
                pp = jc % 2
                if stR[pp]:
                    b.waits("tensor", [stR[pp]])
                if jc >= 2:
                    b.waits("tensor", [vt_ev[jc - 2]])
                vpe = b.op("tensor", lambda e, jc=jc, pp=pp: e.matmul(
                    st2[pp][:, 0:128], rbf(a_v + jc * 128, 128), identb[:],
                    start=True, stop=True), "pe")
                stR[pp] = None
                eng = "scalar" if jc % 2 == 0 else "vector"
                sig = "ss" if eng == "scalar" else "vv"
                b.wait(eng, "pe", vpe[1])
                vt_ev[jc] = b.op(eng, lambda e, jc=jc, pp=pp: _cp(e, 
                    rbf(a_vtok + jc * 128, 128), st2[pp][:, 0:128]), sig)
                stR[pp] = vt_ev[jc]
            last_vt_ev = vt_ev[15]

            b.wait("scalar", "qS", lds[("q", 3)][1])
            sq_c = b.op("scalar", lambda e: e.activation(rbf(a_sq, 2048), rbf(a_q[par], 2048),
                                                         AF.Square), "ss")
            b.wait("tensor", "ss", sq_c[1])
            if smR:
                b.waits("tensor", [smR])
            qn = [None, None]
            for half in range(2):
                b.op("tensor", lambda e, half=half: e.matmul(
                    sm2[half][0:1, :], ones_cb[:], rbf(a_sq + half * 1024, 512),
                    start=True, stop=False))
                qn[half] = b.op("tensor", lambda e, half=half: e.matmul(
                    sm2[half][0:1, :], ones_cb[:], rbf(a_sq + 1024 + half * 512, 512),
                    start=False, stop=True), "pe")
            b.wait("vector", "pe", qn[1][1])
            b.chain("vector", lambda e: e.reduce_max(rows[0:1, 1024:1025], sm2[0][0:1, :], axis=AX.X))
            mq = b.chain("vector", lambda e: e.reduce_max(rows[0:1, 1040:1041], sm2[1][0:1, :],
                                                          axis=AX.X))
            b.wait("scalar", "qS", lds[("k", 3)][1])
            b.wait("scalar", "vv", mq[1])
            sqk = b.op("scalar", lambda e: e.activation(rbf(a_sq, 2048), rbf(a_k[par], 2048),
                                                        AF.Square), "ss")
            b.wait("tensor", "ss", sqk[1])
            b.wait("tensor", "vv", mq[1])
            kn = [None, None]
            for half in range(2):
                b.op("tensor", lambda e, half=half: e.matmul(
                    sm2[half][0:1, :], ones_cb[:], rbf(a_sq + half * 1024, 512),
                    start=True, stop=False))
                kn[half] = b.op("tensor", lambda e, half=half: e.matmul(
                    sm2[half][0:1, :], ones_cb[:], rbf(a_sq + 1024 + half * 512, 512),
                    start=False, stop=True), "pe")
            b.wait("vector", "pe", kn[1][1])
            b.chain("vector", lambda e: e.reduce_max(rows[0:1, 1056:1057], sm2[0][0:1, :], axis=AX.X))
            b.chain("vector", lambda e: e.reduce_max(rows[0:1, 1072:1073], sm2[1][0:1, :], axis=AX.X))
            b.chain("vector", lambda e: e.tensor_tensor(rows[0:1, 1024:1025], rows[0:1, 1024:1025],
                                                        rows[0:1, 1040:1041], ALU.max))
            b.chain("vector", lambda e: e.tensor_tensor(rows[0:1, 1056:1057], rows[0:1, 1056:1057],
                                                        rows[0:1, 1072:1073], ALU.max))
            m2c = b.chain("vector", lambda e: e.tensor_tensor(rows[0:1, 1088:1089],
                                                              rows[0:1, 1024:1025],
                                                              rows[0:1, 1056:1057], ALU.mult))
            b.wait("scalar", "vv", m2c[1])
            b.chain("scalar", lambda e: e.activation(rows[0:1, 1104:1105], rows[0:1, 1088:1089],
                                                     AF.Ln))
            cc = b.chain("scalar", lambda e: e.activation(rows[0:1, 1120:1121], rows[0:1, 1104:1105],
                                                          AF.Exp, scale=0.5))
            b.wait("vector", "ss", cc[1])
            ncr = b.chain("vector", lambda e: e.tensor_scalar(rows[0:1, 1136:1137],
                                                              rows[0:1, 1120:1121], -1.0, None,
                                                              ALU.mult))
            b.wait("tensor", "vv", ncr[1])
            nb = b.op("tensor", lambda e: e.matmul(sm2[0][:, 0:1], ones_rf[:],
                                                   rows[0:1, 1136:1137],
                                                   start=True, stop=True), "pe")
            b.wait("vector", "pe", nb[1])
            negc = b.op("vector", lambda e: _cp(e, smalls[:, 320:321], sm2[0][:, 0:1]), "vv")
            smR = negc

            for ig in range(4):
                igp = ig % 2
                njc = 4 * ig + 4
                exp_d = {}
                for jc in range(njc):
                    tloc = jc - 4 * ig
                    ioff = 128 * tloc if tloc >= 0 else 0
                    stp = jc % 2
                    if stR[stp]:
                        b.waits("tensor", [stR[stp]])
                    smm = b.op("tensor", lambda e, jc=jc, ig=ig, stp=stp, ioff=ioff, par=par:
                               e.matmul(st2[stp][:, ioff:512], rbf(a_k[par] + jc * 128, 128),
                                        rbf(a_q[par] + ig * 512 + ioff, 512 - ioff),
                                        start=True, stop=True), "pe")
                    if tloc >= 0:
                        b.wait("vector", "pe", smm[1])
                        mk = b.op("vector", lambda e, stp=stp, ioff=ioff: e.tensor_tensor(
                            st2[stp][:, ioff:ioff + 128], st2[stp][:, ioff:ioff + 128],
                            maskT[:], ALU.add), "vv")
                        b.wait("scalar", "vv", mk[1])
                    else:
                        b.wait("scalar", "pe", smm[1])
                    zg = None
                    if tloc >= 1:
                        if ptR[stp]:
                            b.waits("gpsimd", [ptR[stp]])
                        zg = b.op("gpsimd", lambda e, stp=stp, ioff=ioff: e.memset(
                            rbf(a_PT[stp], ioff), 0.0), "gg")
                    if ptR[stp]:
                        b.waits("scalar", [ptR[stp]])
                    b.wait("scalar", "vv", negc[1])
                    exp_d[jc] = b.op("scalar", lambda e, stp=stp, ioff=ioff: e.activation(
                        rbf(a_PT[stp] + ioff, 512 - ioff), st2[stp][:, ioff:512], AF.Exp,
                        bias=smalls[:, 320:321]), "ss")
                    stR[stp] = exp_d[jc]
                    b.wait("tensor", "ss", exp_d[jc][1])
                    if zg:
                        b.wait("tensor", "gg", zg[1])
                    if jc == 0:
                        if avR[igp]:
                            b.waits("tensor", [avR[igp]])
                        if hs == 0 and ig == 0:
                            b.wait("tensor", "qS", spill[1])
                    b.waits("tensor", [vt_ev[jc]])
                    b.op("tensor", lambda e, jc=jc, igp=igp, stp=stp: e.matmul(
                        av2[igp][:], rbf(a_vtok + jc * 128, 128), rbf(a_PT[stp], 512),
                        start=(jc == 0), stop=(jc == njc - 1)))
                    lmm = b.op("tensor", lambda e, jc=jc, igp=igp, stp=stp: e.matmul(
                        l2[igp][0:1, :], ones_cb[:], rbf(a_PT[stp], 512),
                        start=(jc == 0), stop=(jc == njc - 1)), "pe")
                    ptR[stp] = lmm
                b.wait("vector", "pe", ptR[(njc - 1) % 2][1])
                b.chain("vector", lambda e, igp=igp: e.tensor_scalar(
                    rows[0:1, 0:512], l2[igp][0:1, :], 1e-35, None, ALU.add))
                rr = b.chain("vector", lambda e: e.reciprocal(rows[0:1, 512:1024],
                                                              rows[0:1, 0:512]))
                if smR:
                    b.waits("tensor", [smR])
                b.wait("tensor", "vv", rr[1])
                rbp = b.op("tensor", lambda e, igp=igp: e.matmul(
                    sm2[igp][:], ones_rf[:], rows[0:1, 512:1024], start=True, stop=True), "pe")
                b.wait("vector", "pe", rbp[1])
                rbc = b.op("vector", lambda e, igp=igp: _cp(e, rf32(a_rb, 512),
                                                                      sm2[igp][:]), "vv")
                ave = b.op("vector", lambda e, igp=igp, h=h, ig=ig: e.tensor_tensor(
                    rbf(a_attnoT + h * 2048 + ig * 512, 512), av2[igp][:], rf32(a_rb, 512),
                    ALU.mult), "vv")
                avR[igp] = ave
                smR = rbc
            hs_end[hs] = {"pe": b.cnt["pe"], "vv": b.cnt["vv"], "ss": b.cnt["ss"]}

        if debug:
            dbg_aot = nc.dram_tensor(f"dbg_aot{s}", [128, 32768], BF16,
                                     kind="ExternalOutput").ap()
            b.dma("sync", dbg_aot, rbf(a_attnoT, 32768),
                  waits=[("vv", hs_end[s * HEADS + HEADS - 1]["vv"])])
        bld = b.dma("sync", rf32(a_bb, 2048)[0:1, :], bout_h,
                    waits=[("vv", hs_end[s * HEADS + HEADS - 1]["vv"])])
        b.wait("tensor", "qS", bld[1])
        b.wait("tensor", "qG", wout_ld[1])
        if smR:
            b.waits("tensor", [smR])
        for ig in range(2):
            if avR[ig]:
                b.waits("tensor", [avR[ig]])
        bbp = []
        for n in range(4):
            bbp.append(b.op("tensor", lambda e, n=n: e.matmul(
                sm2[n % 2][:], ones_rf[:], rf32(a_bb, 2048)[0:1, n * 512:(n + 1) * 512],
                start=True, stop=True), "pe"))
        bbc = None
        for n in range(4):
            b.wait("vector", "pe", bbp[n][1])
            bbc = b.op("vector", lambda e, n=n: _cp(e, 
                rf32(a_bb + 1024 * n, 512), sm2[n % 2][:]), "vv")
        es_a.close()
        es_p = ExitStack()
        pj = [es_p.enter_context(nc.psum_tensor(f"pj{s}_{i}", [128, 2048], F32)) for i in range(2)]
        pj_ev = [None, None]
        for t in range(16):
            tpar = t % 2
            gidx = s * 16 + t
            if t == 0:
                b.wait("tensor", "vv", bbc[1])
            if pj_ev[tpar]:
                b.waits("tensor", [pj_ev[tpar]])
            pj_pe = None
            for k in range(KC):
                for n in range(4):
                    last = (k == KC - 1 and n == 3)
                    r = b.op("tensor", lambda e, k=k, n=n, t=t, tpar=tpar: e.matmul(
                        pj[tpar][:, n * 512:(n + 1) * 512],
                        rbf(a_attnoT + k * 2048 + t * 128, 128),
                        rbf(a_wout + k * 2048 + n * 512, 512),
                        start=(k == 0), stop=(k == KC - 1)), "pe" if last else None)
                    if last:
                        pj_pe = r
            b.wait("vector", "pe", pj_pe[1])
            if gidx >= 2:
                b.wait("vector", "qS", proj_store[gidx - 2][1])
            pj_ev[tpar] = b.op("vector", lambda e, tpar=tpar: e.tensor_tensor(
                rf32(a_sst[tpar], 2048), pj[tpar][:], rf32(a_bb, 2048), ALU.add), "vv")
            proj_store[gidx] = b.dma("sync", s_s[gidx], rf32(a_sst[tpar], 2048),
                                     waits=[pj_ev[tpar]])
        if debug:
            dbg_bb = nc.dram_tensor(f"dbg_bb{s}", [128, 2048], F32,
                                    kind="ExternalOutput").ap()
            b.dma("sync", dbg_bb, rf32(a_bb, 2048), waits=[pj_ev[1]])
        prev_proj_ev = pj_ev[1]
        es_p.close()

    if upto == "proj":
        b.wait("sync", "qS", proj_store[NT128 - 1][1])
        blk = es.enter_context(nc.Block())
        b.emit_all(blk)
        es.close()
        return nc

    q_chunk = [0, 8192]
    q_aa = 16384
    q_qq = [24576, 26624]
    q_part = 28672

    es_qz = ExitStack()
    qt_ps = es_qz.enter_context(nc.psum_tensor("qt_ps", [128, 512], F32))
    lp_vv = b.cnt["vv"]
    lp_st = proj_store[NT128 - 1]
    last_q_store = None
    for wi in range(3):
        wh = (w1_h, w2_h, w3_h)[wi]
        qs_ = qw_s[wi]
        flat = wh.rearrange("a b c d -> (a b) (c d)") if wi < 2 else wh
        if wi == 0:
            b.waits("gpsimd", [("vv", lp_vv), ("qS", lp_st[1])])
        else:
            b.wait("gpsimd", "pe", gs[1])
        zq = b.op("gpsimd", lambda e: e.memset(rf32(q_part, 720), 0.0), "gg")
        b.wait("vector", "gg", zq[1])
        red = {}
        for ci in range(MCF):
            par = ci % 2
            if wi == 0 and ci < 2:
                w = [("vv", lp_vv), ("qS", lp_st[1])]
            elif ci < 2:
                w = [("vv", qq_d[MCF - 2 + ci][1])]
            else:
                w = [("vv", red[ci - 2][1])]
            ld = b.dma("sync", rf32(q_chunk[par], 2048), flat[ci * 128:(ci + 1) * 128, :],
                       waits=w)
            b.wait("vector", "qS", ld[1])
            red[ci] = b.op("vector", lambda e, ci=ci, par=par: e.tensor_reduce(
                rf32(q_part, 720)[:, 16 * ci:16 * ci + 1], rf32(q_chunk[par], 2048), AX.X,
                ALU.add, apply_absolute_value=True), "vv")
        b.wait("vector", "vv", red[MCF - 1][1])
        p1 = b.chain("vector", lambda e: e.reduce_sum(rf32(q_part, 720)[:, 704:705],
                                                      rf32(q_part, 720)[:, 0:704], axis=AX.X))
        b.wait("tensor", "vv", p1[1])
        gs = b.op("tensor", lambda e: e.matmul(qt_ps[0:1, 0:1], rf32(q_part, 720)[:, 704:705],
                                               ones_cf[:], start=True, stop=True), "pe")
        b.wait("vector", "pe", gs[1])
        b.chain("vector", lambda e: e.tensor_scalar(rows[0:1, 1168:1169], qt_ps[0:1, 0:1],
                                                    0.5 / N_W, None, ALU.mult))
        thr = b.chain("vector", lambda e: e.tensor_scalar(rows[0:1, 1168:1169],
                                                          rows[0:1, 1168:1169], 0.5e-8, None,
                                                          ALU.add))
        b.wait("tensor", "vv", thr[1])
        thb = b.op("tensor", lambda e: e.matmul(qt_ps[:, 0:1], ones_rf[:],
                                                rows[0:1, 1168:1169], start=True, stop=True), "pe")
        b.wait("vector", "pe", thb[1])
        thbc = b.op("vector", lambda e: _cp(e, smalls[:, 336:337], qt_ps[:, 0:1]), "vv")
        qq_d = {}
        qst = {}
        for ci in range(MCF):
            par = ci % 2
            w = [("vv", thbc[1])] if ci < 2 else [("vv", qq_d[ci - 2][1])]
            ld = b.dma("sync", rf32(q_chunk[par], 2048), flat[ci * 128:(ci + 1) * 128, :],
                       waits=w)
            b.wait("vector", "qS", ld[1])
            if ci >= 2:
                b.wait("vector", "qS", qst[ci - 2][1])
            b.op("vector", lambda e, par=par: e.tensor_scalar(
                rf32(q_aa, 2048), rf32(q_chunk[par], 2048), smalls[:, 336:337], 0.0,
                ALU.add, ALU.is_lt))
            qq_d[ci] = b.op("vector", lambda e, par=par: e.scalar_tensor_tensor(
                rbf(q_qq[par], 2048), rf32(q_chunk[par], 2048), smalls[:, 336:337],
                rf32(q_aa, 2048), ALU.is_gt, ALU.subtract), "vv")
            qst[ci] = b.dma("sync", qs_[ci], rbf(q_qq[par], 2048), waits=[qq_d[ci]])
        last_q_store = qst[MCF - 1]
    es_qz.close()

    f_gsb = 0
    f_xnT = 32768
    f_hid = 49152
    f_qw1 = [71680, 73728]
    f_qw2 = [75776, 77824]
    f_qw3 = [79872, 80896]
    f_sig = [83968, 84992]
    f_st = 86016
    f_xt = 90112
    f_sgm = 94208

    es_f = ExitStack()
    ps4 = [es_f.enter_context(nc.psum_tensor(f"fps{i}", [128, 1024], F32)) for i in range(4)]

    dbg_hid = nc.dram_tensor("dbg_hid", [MCF, 128, 1024], BF16, kind=dk).ap() if debug else None
    dbg_gsb = nc.dram_tensor("dbg_gsb", [8, 128, 2048], F32, kind=dk).ap() if debug else None
    ffn3_pe = None
    prev_gev = None
    out_store = [None] * NT128
    hid_d = {}
    x12_d = {}
    g_ev = {}

    for tt in range(4):
        s, tl = divmod(tt, 2)
        xn_lds = None
        for k in range(KC):
            w = []
            if k == 0:
                w = [("pe", (x12_d[(tt - 1, 1, 21)][1] if tt > 0 else 0))]
                if tt == 0:
                    w.append(("qS", last_q_store[1]))
            xn_lds = b.dma("sync", rbf(f_xnT + k * 1024, 1024),
                           xnT_s[k][:, tt * 1024:(tt + 1) * 1024], waits=w)
        for half in range(2):
            mbase = 22 * half
            qw12_ld = {}
            for ml in range(23):
                if ml < 22:
                    m = mbase + ml
                    par = ml % 2
                    if ml >= 2:
                        w1w = [("pe", x12_d[(tt, half, ml - 2)][1])]
                    elif tt == 0 and half == 0:
                        w1w = [("qS", last_q_store[1])]
                    else:
                        w1w = ()
                    b.dma("gpsimd", rbf(f_qw1[par], 2048), qw_s[0][m], waits=w1w)
                    qw12_ld[ml] = b.dma("gpsimd", rbf(f_qw2[par], 2048), qw_s[1][m])
                if ml == 0:
                    continue
                ml, m, par = ml - 1, mbase + ml - 1, (ml - 1) % 2
                b.wait("tensor", "qG", qw12_ld[ml][1])
                if ml == 0 and half == 0:
                    b.wait("tensor", "qS", xn_lds[1])
                if ml < 2:
                    if prev_gev:
                        b.waits("tensor", [prev_gev])
                elif ml >= 2:
                    b.wait("tensor", "vv", hid_d[(tt, half, ml - 2)][1])
                for k in range(KC):
                    for n in range(2):
                        b.op("tensor", lambda e, par=par, k=k, n=n: e.matmul(
                            ps4[par][:, n * 512:(n + 1) * 512], rbf(f_qw1[par] + k * 128, 128),
                            rbf(f_xnT + k * 1024 + n * 512, 512),
                            start=(k == 0), stop=(k == KC - 1)))
                for k in range(KC):
                    for n in range(2):
                        last = (k == KC - 1 and n == 1)
                        r = b.op("tensor", lambda e, par=par, k=k, n=n: e.matmul(
                            ps4[2 + par][:, n * 512:(n + 1) * 512], rbf(f_qw2[par] + k * 128, 128),
                            rbf(f_xnT + k * 1024 + n * 512, 512),
                            start=(k == 0), stop=(k == KC - 1)), "pe" if last else None)
                        if last:
                            x12_d[(tt, half, ml)] = r
                b.wait("scalar", "pe", x12_d[(tt, half, ml)][1])
                if ml >= 2:
                    b.wait("scalar", "vv", hid_d[(tt, half, ml - 2)][1])
                sg = b.op("scalar", lambda e, par=par: e.activation(
                    rbf(f_sig[par], 1024), ps4[par][:], AF.Sigmoid), "ss")
                b.wait("vector", "ss", sg[1])
                hid_d[(tt, half, ml)] = b.op("vector", lambda e, par=par, ml=ml: e.tensor_tensor(
                    rbf(f_hid + ml * 1024, 1024), ps4[2 + par][:], rbf(f_sig[par], 1024),
                    ALU.mult), "vv")
                if debug and tt == 0:
                    b.dma("sync", dbg_hid[mbase + ml], rbf(f_hid + ml * 1024, 1024),
                          waits=[hid_d[(tt, half, ml)]])
            b.wait("tensor", "vv", hid_d[(tt, half, 21)][1])
            for quad in range(2):
                for nh in range(2):
                    ml_pe = {}
                    qw3_ld = {}
                    for ml in range(23):
                        if ml < 22:
                            m = mbase + ml
                            par = ml % 2
                            if ml >= 2:
                                w = [("pe", ml_pe[ml - 2][1])]
                            elif ffn3_pe is not None:
                                w = [("pe", ffn3_pe[1])]
                            else:
                                w = [("qS", last_q_store[1])]
                            qw3_ld[ml] = b.dma("gpsimd", rbf(f_qw3[par], 1024),
                                               qw_s[2][m][:, nh * 1024:(nh + 1) * 1024],
                                               waits=w)
                        if ml == 0:
                            continue
                        ml, m, par = ml - 1, mbase + ml - 1, (ml - 1) % 2
                        b.wait("tensor", "qG", qw3_ld[ml][1])
                        if ml == 0 and prev_gev:
                            b.waits("tensor", [prev_gev])
                        for t4 in range(4):
                            t128 = quad * 4 + t4
                            for nn in range(2):
                                lastml = (t4 == 3 and nn == 1)
                                r = b.op("tensor", lambda e, ml=ml, par=par, t4=t4, t128=t128,
                                         nn=nn: e.matmul(
                                             ps4[t4][:, nn * 512:(nn + 1) * 512],
                                             rbf(f_hid + ml * 1024 + t128 * 128, 128),
                                             rbf(f_qw3[par] + nn * 512, 512),
                                             start=(ml == 0), stop=(ml == 21)),
                                         "pe" if lastml else None)
                                if lastml:
                                    ml_pe[ml] = r
                    ffn3_pe = ml_pe[21]
                    for t4 in range(4):
                        t128 = quad * 4 + t4
                        dst = rf32(f_gsb + t128 * 4096 + nh * 2048, 1024)
                        b.wait("vector", "pe", ffn3_pe[1])
                        if half == 0 and tt > 0:
                            b.wait("vector", "qS", out_store[(tt - 1) * 8 + t128][1])
                        if half == 0:
                            ev = b.op("vector", lambda e, t4=t4, dst=dst: _cp(e, 
                                dst, ps4[t4][:]), "vv")
                        else:
                            ev = b.op("vector", lambda e, t4=t4, dst=dst: e.tensor_tensor(
                                dst, dst, ps4[t4][:], ALU.add), "vv")
                            g_ev[(tt, t128, nh)] = ev
                    prev_gev = ev
        dbg_gd = None
        if debug and tt == 0:
            for t128 in range(8):
                dbg_gd = b.dma("sync", dbg_gsb[t128], rf32(f_gsb + t128 * 4096, 2048),
                               waits=[g_ev[(tt, t128, 0)], g_ev[(tt, t128, 1)]])
        sgm_rd = None
        for t128 in range(8):
            gt = tt * 8 + t128
            trow = tl * 8 + t128
            gsb = rf32(f_gsb + t128 * 4096, 2048)
            sld = b.dma("sync", rf32(f_st, 2048), s_s[s * 16 + trow],
                        waits=[g_ev[(tt, t128, 1)]] if True else ())
            xld = b.dma("sync", rf32(f_xt, 2048), x_h[s, trow * 128:(trow + 1) * 128, :])
            b.waits("scalar", [g_ev[(tt, t128, 0)], g_ev[(tt, t128, 1)]])
            if sgm_rd:
                b.wait("scalar", "vv", sgm_rd[1])
            if dbg_gd and t128 == 0:
                b.wait("scalar", "qS", dbg_gd[1])
            sgc = b.op("scalar", lambda e, gsb=gsb: e.activation(
                rbf(f_sgm, 2048), gsb, AF.Sigmoid), "ss")
            b.wait("vector", "ss", sgc[1])
            sgm_rd = b.chain("vector", lambda e, gsb=gsb: e.tensor_tensor(
                gsb, gsb, rbf(f_sgm, 2048), ALU.mult))
            b.wait("vector", "qS", sld[1])
            b.chain("vector", lambda e, gsb=gsb: e.tensor_tensor(gsb, gsb, rf32(f_st, 2048),
                                                                 ALU.mult))
            b.wait("vector", "qS", xld[1])
            fin = b.chain("vector", lambda e, gsb=gsb: e.tensor_tensor(
                gsb, gsb, rf32(f_xt, 2048), ALU.add))
            out_store[gt] = b.dma("sync", out_h[s, trow * 128:(trow + 1) * 128, :], gsb,
                                  waits=[fin])
    es_f.close()

    b.wait("sync", "qS", out_store[NT128 - 1][1])
    b.wait("gpsimd", "gg", g_ones[1])

    blk = es.enter_context(nc.Block())
    b.emit_all(blk)
    es.close()
    return nc


def _prep_weights(wqkv, wout, bout, w1, w2, w3):
    wqkv = np.ascontiguousarray(
        np.asarray(wqkv, dtype=np.float32).reshape(KC, 128, MCQ, 128).transpose(2, 1, 0, 3))
    wout = np.ascontiguousarray(np.asarray(wout, dtype=np.float32))
    bout = np.ascontiguousarray(np.asarray(bout, dtype=np.float32).reshape(1, D))
    w1 = np.ascontiguousarray(
        np.asarray(w1, dtype=np.float32).T.reshape(KC, 128, MCF, 128).transpose(2, 1, 0, 3))
    w2 = np.ascontiguousarray(
        np.asarray(w2, dtype=np.float32).T.reshape(KC, 128, MCF, 128).transpose(2, 1, 0, 3))
    w3 = np.ascontiguousarray(np.asarray(w3, dtype=np.float32).T)
    return wqkv, wout, bout, w1, w2, w3


def kernel(L0_attn_x, L0_attn_w_qkv, L0_attn_w_out, L0_attn_b_out,
           L0_ffn_w1, L0_ffn_w2, L0_ffn_w3):
    x = np.ascontiguousarray(np.asarray(L0_attn_x, dtype=np.float32))
    wqkv, wout, bout, w1, w2, w3 = _prep_weights(
        L0_attn_w_qkv, L0_attn_w_out, L0_attn_b_out, L0_ffn_w1, L0_ffn_w2, L0_ffn_w3)
    nc = build_program()
    in_maps = [{"x": np.ascontiguousarray(x[2 * c:2 * c + 2]),
                "wqkv": wqkv, "wout": wout, "bout": bout,
                "w1": w1, "w2": w2, "w3": w3} for c in range(8)]
    res = run_bass_kernel_spmd(nc, in_maps, core_ids=list(range(8)))
    return np.concatenate([r["out"] for r in res.results], axis=0)


# version 05 (final)
# speedup vs baseline: 1.2503x; 1.1476x over previous
import math
import numpy as np
from contextlib import ExitStack

import concourse.bass as bass
import concourse.mybir as mybir
from concourse.bass_utils import run_bass_kernel_spmd

F32 = mybir.dt.float32
BF16 = mybir.dt.bfloat16
I32 = mybir.dt.int32
AF = mybir.ActivationFunctionType
ALU = mybir.AluOpType
AX = mybir.AxisListType

SEQ, T, D, KC, HEADS, DH = 2, 2048, 2048, 16, 16, 128
MCQ, HF, MCF = 48, 5632, 44
TOK, NT128 = SEQ * T, 32
QSCALE = float(1.0 / math.sqrt(DH))
NEG = -1e30
N_W = float(HF * D)

ENGS = ("sync", "gpsimd", "vector", "scalar", "tensor")


def _cp(e, dst, src):
    f = getattr(e, "tensor_copy", None) or e.copy
    return f(dst, src)


class Bld:
    def __init__(self, nc, es):
        self.nc = nc
        self.ev = {e: [] for e in ENGS}
        self.cnt = {}
        self.h = {}
        for name in ("qS", "qG", "qA", "pe", "vv", "ss", "gg"):
            self.h[name] = es.enter_context(nc.semaphore(name))
            self.cnt[name] = 0

    def on(self, eng, fn):
        self.ev[eng].append(fn)

    def wait(self, eng, sem, val):
        if not val:
            return
        h = self.h[sem]
        self.on(eng, lambda e, h=h, v=val: e.wait_ge(h, v))

    def waits(self, eng, pairs):
        for sem, val in pairs:
            self.wait(eng, sem, val)

    def op(self, eng, emit, sig=None, n=1):
        if sig is None:
            self.on(eng, emit)
            return None
        self.cnt[sig] += n
        v = self.cnt[sig]
        h = self.h[sig]
        self.on(eng, lambda e, emit=emit, h=h, n=n: emit(e).then_inc(h, n))
        return (sig, v)

    def chain(self, eng, emit):
        sig = {"vector": "vv", "scalar": "ss", "gpsimd": "gg", "tensor": "pe"}[eng]
        r = self.op(eng, emit, sig)
        self.wait(eng, sig, r[1])
        return r

    def dma(self, q, out, in_, waits=()):
        self.waits(q, waits)
        sig = "qS" if q == "sync" else "qG"
        return self.op(q, lambda e, o=out, i=in_: e.dma_start(out=o, in_=i), sig, 16)

    def emit_all(self, blk):
        b = self

        @blk.sync
        def _(e):
            for f in b.ev["sync"]:
                f(e)

        @blk.gpsimd
        def _(e):
            for f in b.ev["gpsimd"]:
                f(e)

        @blk.vector
        def _(e):
            for f in b.ev["vector"]:
                f(e)

        @blk.scalar
        def _(e):
            for f in b.ev["scalar"]:
                f(e)

        @blk.tensor
        def _(e):
            for f in b.ev["tensor"]:
                f(e)


def build_program(debug=False, upto="all"):
    nc = bass.Bass("TRN2", target_bir_lowering=False)
    es = ExitStack()
    dk = "ExternalOutput" if debug else "Internal"

    x_h = nc.dram_tensor("x", [SEQ, T, D], F32, kind="ExternalInput").ap()
    wqkv_h = nc.dram_tensor("wqkv", [MCQ, 128, KC, 128], F32, kind="ExternalInput").ap()
    wout_h = nc.dram_tensor("wout", [D, D], F32, kind="ExternalInput").ap()
    bout_h = nc.dram_tensor("bout", [1, D], F32, kind="ExternalInput").ap()
    w1_h = nc.dram_tensor("w1", [MCF, 128, KC, 128], F32, kind="ExternalInput").ap()
    w2_h = nc.dram_tensor("w2", [MCF, 128, KC, 128], F32, kind="ExternalInput").ap()
    w3_h = nc.dram_tensor("w3", [HF, D], F32, kind="ExternalInput").ap()
    out_h = nc.dram_tensor("out", [SEQ, T, D], F32, kind="ExternalOutput").ap()

    xnT_s = nc.dram_tensor("xnT_s", [KC, 128, TOK], BF16, kind=dk).ap()
    qkv_s = nc.dram_tensor("qkv_s", [MCQ, 8, 128, 512], BF16, kind=dk).ap()
    s_s = nc.dram_tensor("s_s", [NT128, 128, D], F32, kind=dk).ap()
    qw_s = [nc.dram_tensor(f"qw{i}_s", [MCF, 128, D], BF16, kind=dk).ap() for i in (1, 2, 3)]

    identb = es.enter_context(nc.sbuf_tensor("identb", [128, 128], BF16))
    maskT = es.enter_context(nc.sbuf_tensor("maskT", [128, 128], F32))
    ones_cb = es.enter_context(nc.sbuf_tensor("ones_cb", [128, 1], BF16))
    ones_cf = es.enter_context(nc.sbuf_tensor("ones_cf", [128, 1], F32))
    ones_rf = es.enter_context(nc.sbuf_tensor("ones_rf", [1, 128], F32))
    smalls = es.enter_context(nc.sbuf_tensor("smalls", [128, 512], F32))
    rows = es.enter_context(nc.sbuf_tensor("rows", [1, 1536], F32))

    AR = 98304
    arena = es.enter_context(nc.sbuf_tensor("arena", [128, AR], BF16))

    def rbf(off, n):
        return arena[:, off:off + n]

    def rf32(off, nf32):
        return arena[:, off:off + 2 * nf32].bitcast(F32)

    b = Bld(nc, es)

    iotp = rbf(90112, 256).bitcast(I32)
    iotf = rbf(90624, 256).bitcast(I32)
    b.op("gpsimd", lambda e: e.iota(iotf, pattern=[[1, 128]], base=0, channel_multiplier=0))
    g_iot = b.op("gpsimd", lambda e: e.iota(iotp, pattern=[[0, 128]], base=0,
                                            channel_multiplier=1), "gg")
    b.op("gpsimd", lambda e: e.memset(ones_cb[:], 1.0))
    b.op("gpsimd", lambda e: e.memset(ones_cf[:], 1.0))
    g_ones = b.op("gpsimd", lambda e: e.memset(ones_rf[:], 1.0), "gg")
    b.wait("vector", "gg", g_iot[1])
    b.op("vector", lambda e: e.tensor_tensor(identb[:], iotf, iotp, ALU.is_equal))
    b.op("vector", lambda e: e.tensor_tensor(maskT[:], iotp, iotf, ALU.is_gt))
    v_ini = b.op("vector", lambda e: e.tensor_scalar(maskT[:], maskT[:], NEG, None, ALU.mult), "vv")

    o_xnT = 0
    o_xt = [65536, 65536 + 8192]
    o_scrap = 65536 + 16384
    o_xn = [o_scrap + 4096, o_scrap + 6144]
    o_wq = [o_scrap + 8192, o_scrap + 10240]
    o_stage = o_scrap + 12288

    es_ln = ExitStack()
    ln_tp = [es_ln.enter_context(nc.psum_tensor(f"ln_tp{i}", [128, 128], F32)) for i in range(2)]

    xn_done = [None] * NT128
    tp_last = {}
    pe_tp = {}
    for i in range(NT128):
        s, t = divmod(i, 16)
        par = i % 2
        xt = rf32(o_xt[par], 2048)
        xn = rbf(o_xn[par], 2048)
        sm = par * 160
        xld = b.dma("sync", xt, x_h[s, t * 128:(t + 1) * 128, :],
                    waits=[("ss", xn_done[i - 2][1])] if i >= 2 else ())
        b.wait("vector", "qS", xld[1])
        if i >= 2:
            b.wait("vector", "ss", xn_done[i - 2][1])
        b.chain("vector", lambda e, xt=xt, sm=sm: e.reduce_sum(smalls[:, sm:sm + 1], xt, axis=AX.X))
        b.wait("scalar", "qS", xld[1])
        if i == 0:
            b.wait("scalar", "vv", v_ini[1])
        sq = b.op("scalar", lambda e, xt=xt, sm=sm: e.activation(
            rf32(o_scrap, 2048), xt, AF.Square, accum_out=smalls[:, sm + 128:sm + 128 + 1]), "ss")
        b.wait("vector", "ss", sq[1])
        b.chain("vector", lambda e, sm=sm: e.tensor_scalar(
            smalls[:, sm + 16:sm + 16 + 1], smalls[:, sm:sm + 1], -1.0 / D, None, ALU.mult))
        b.chain("vector", lambda e, sm=sm: e.tensor_scalar(
            smalls[:, sm + 32:sm + 32 + 1], smalls[:, sm + 128:sm + 128 + 1], 1.0 / D, None, ALU.mult))
        b.chain("vector", lambda e, sm=sm: e.tensor_tensor(
            smalls[:, sm + 48:sm + 48 + 1], smalls[:, sm + 16:sm + 16 + 1], smalls[:, sm + 16:sm + 16 + 1], ALU.mult))
        b.chain("vector", lambda e, sm=sm: e.tensor_tensor(
            smalls[:, sm + 64:sm + 64 + 1], smalls[:, sm + 32:sm + 32 + 1], smalls[:, sm + 48:sm + 48 + 1],
            ALU.subtract))
        var_c = b.chain("vector", lambda e, sm=sm: e.tensor_scalar(
            smalls[:, sm + 64:sm + 64 + 1], smalls[:, sm + 64:sm + 64 + 1], 1e-5, None, ALU.add))
        b.wait("scalar", "vv", var_c[1])
        sd_c = b.op("scalar", lambda e, sm=sm: e.activation(
            smalls[:, sm + 80:sm + 80 + 1], smalls[:, sm + 64:sm + 64 + 1], AF.Sqrt), "ss")
        b.wait("vector", "ss", sd_c[1])
        b.chain("vector", lambda e, sm=sm: e.reciprocal(smalls[:, sm + 96:sm + 96 + 1],
                                                     smalls[:, sm + 80:sm + 80 + 1]))
        nmr_c = b.chain("vector", lambda e, sm=sm: e.tensor_tensor(
            smalls[:, sm + 112:sm + 112 + 1], smalls[:, sm + 16:sm + 16 + 1], smalls[:, sm + 96:sm + 96 + 1],
            ALU.mult))
        b.wait("scalar", "vv", nmr_c[1])
        if i >= 2:
            b.wait("scalar", "pe", pe_tp[(i - 2, KC - 1)][1])
        xn_done[i] = b.op("scalar", lambda e, xt=xt, xn=xn, sm=sm: e.activation(
            xn, xt, AF.Identity, scale=smalls[:, sm + 96:sm + 96 + 1],
            bias=smalls[:, sm + 112:sm + 112 + 1]), "ss")
        b.wait("tensor", "ss", xn_done[i][1])
        for k in range(KC):
            pp = k % 2
            if pp in tp_last:
                b.waits("tensor", [tp_last[pp]])
            pe_tp[(i, k)] = b.op("tensor", lambda e, xn=xn, k=k, pp=pp: e.matmul(
                ln_tp[pp][:], xn[:, k * 128:(k + 1) * 128], identb[:],
                start=True, stop=True), "pe")
            eng = "scalar" if (k % 4) < 2 else "vector"
            sig = "ss" if eng == "scalar" else "vv"
            b.wait(eng, "pe", pe_tp[(i, k)][1])
            dst = rbf(o_xnT + k * 4096 + i * 128, 128)
            tp_last[pp] = b.op(eng, lambda e, pp=pp, dst=dst: _cp(e, dst, ln_tp[pp][:]), sig)

    ln_end = {"ss": b.cnt["ss"], "vv": b.cnt["vv"]}
    es_ln.close()

    if upto == "ln":
        dbg_sm = nc.dram_tensor("dbg_sm", [128, 512], F32, kind="ExternalOutput").ap()
        dbg_xn = nc.dram_tensor("dbg_xn", [128, 2048], BF16, kind="ExternalOutput").ap()
        dbg_xt = nc.dram_tensor("dbg_xt", [128, 2048], F32, kind="ExternalOutput").ap()
        sp = None
        for k in range(KC):
            sp = b.dma("sync", xnT_s[k], rbf(o_xnT + k * 4096, 4096),
                       waits=[("ss", ln_end["ss"]), ("vv", ln_end["vv"])] if k == 0 else ())
        sp = b.dma("sync", dbg_sm, smalls[:])
        sp = b.dma("sync", dbg_xn, rbf(o_xn[0], 2048))
        sp = b.dma("sync", dbg_xt, rf32(o_xt[0], 2048))
        b.wait("sync", "qS", sp[1])
        blk = es.enter_context(nc.Block())
        b.emit_all(blk)
        es.close()
        return nc

    es_q = ExitStack()
    qkv_ps = [es_q.enter_context(nc.psum_tensor(f"qkv_ps{i}", [128, 512], F32)) for i in range(8)]

    evac_done = {}
    store_done = {}
    pe_m = [None] * MCQ
    for m in range(MCQ):
        par = m % 2
        wq = rbf(o_wq[par], 2048)
        wld = b.dma("gpsimd", wq, wqkv_h[m].rearrange("p k c -> p (k c)"),
                    waits=[("pe", pe_m[m - 2][1])] if m >= 2 else ())
        b.wait("tensor", "qG", wld[1])
        if m == 0:
            b.wait("tensor", "ss", ln_end["ss"])
            b.wait("tensor", "vv", ln_end["vv"])
        if m >= 1:
            for ti in range(8):
                b.waits("tensor", [evac_done[(m - 1, ti)]])
        for k in range(KC):
            for ti in range(8):
                last = (k == KC - 1 and ti == 7)
                r = b.op("tensor", lambda e, wq=wq, k=k, ti=ti: e.matmul(
                    qkv_ps[ti][:], wq[:, k * 128:(k + 1) * 128],
                    rbf(o_xnT + k * 4096 + ti * 512, 512),
                    start=(k == 0), stop=(k == KC - 1)), "pe" if last else None)
                if last:
                    pe_m[m] = r
        for ti in range(8):
            eng = "scalar" if ti < 4 else "vector"
            sig = "ss" if eng == "scalar" else "vv"
            stg = rbf(o_stage + (ti % 4) * 512, 512)
            b.wait(eng, "pe", pe_m[m][1])
            prev = store_done.get((m, ti - 4)) or store_done.get((m - 1, ti + 4 - 8))
            if ti >= 4:
                prev = store_done.get((m, ti - 4))
            else:
                prev = store_done.get((m - 1, ti + 4))
            if prev:
                b.wait(eng, "qS", prev[1])
            if m < 16 and eng == "scalar":
                cnt = b.op(eng, lambda e, ti=ti, stg=stg: e.activation(
                    stg, qkv_ps[ti][:], AF.Identity, scale=QSCALE), sig)
            elif m < 16:
                cnt = b.op(eng, lambda e, ti=ti, stg=stg: e.tensor_scalar(
                    stg, qkv_ps[ti][:], QSCALE, None, ALU.mult), sig)
            else:
                cnt = b.op(eng, lambda e, ti=ti, stg=stg: _cp(e, stg, qkv_ps[ti][:]), sig)
            evac_done[(m, ti)] = cnt
            store_done[(m, ti)] = b.dma("sync", qkv_s[m, ti], stg, waits=[cnt])
    qkv_end = {"ss": b.cnt["ss"], "vv": b.cnt["vv"]}
    es_q.close()

    if upto == "qkv":
        b.wait("sync", "qS", store_done[(MCQ - 1, 7)][1])
        sp = None
        for k in range(KC):
            sp = b.dma("sync", xnT_s[k], rbf(o_xnT + k * 4096, 4096),
                       waits=[("pe", pe_m[MCQ - 1][1])] if k == 0 else ())
        b.wait("sync", "qS", sp[1])
        blk = es.enter_context(nc.Block())
        b.emit_all(blk)
        es.close()
        return nc

    spill = None
    for k in range(KC):
        spill = b.dma("sync", xnT_s[k], rbf(o_xnT + k * 4096, 4096),
                      waits=[("pe", pe_m[MCQ - 1][1])] if k == 0 else ())

    a_wout, a_attnoT = 0, 32768
    a_q = [65536, 67584]
    a_k = [69632, 71680]
    a_v = 73728
    a_vtok = 75776
    a_PT = [77824, 78336]
    a_sq = 78848
    a_rb = 80896
    a_sst = [81920, 90112]
    a_bb = 86016

    dbg_negc = nc.dram_tensor("dbg_negc", [32, 128, 1], F32, kind="ExternalOutput").ap() \
        if debug else None
    dbg_rows = nc.dram_tensor("dbg_rows", [32, 1, 1536], F32, kind="ExternalOutput").ap() \
        if debug else None
    wout_ld = None
    for k in range(KC):
        wout_ld = b.dma("gpsimd", rbf(a_wout + k * 2048, 2048),
                        wout_h[k * 128:(k + 1) * 128, :],
                        waits=[("qS", spill[1])] if k == 0 else ())

    proj_store = [None] * NT128
    hs_end = {}
    stR = [None, None]
    ptR = [None, None]
    avR = [None, None]
    smR = None
    last_vt_ev = None

    prev_proj_ev = None
    for s in range(SEQ):
        es_a = ExitStack()
        st2 = [es_a.enter_context(nc.psum_tensor(f"st{s}_{i}", [128, 512], F32)) for i in range(4)]
        avl = [es_a.enter_context(nc.psum_tensor(f"avl{s}_{i}", [128, 1024], F32)) for i in range(2)]
        av2 = [avl[0][:, 0:512], avl[1][:, 0:512]]
        l2 = [avl[0][0:1, 512:1024], avl[1][0:1, 512:1024]]
        sm2 = [avl[0][:, 512:1024], avl[1][:, 512:1024]]
        if s == 0:
            b.wait("tensor", "ss", qkv_end["ss"])
            b.wait("tensor", "vv", qkv_end["vv"])
            for eng in ("tensor", "vector", "scalar", "gpsimd"):
                b.wait(eng, "qS", b.cnt["qS"])
                b.wait(eng, "qG", b.cnt["qG"])
        stR = [prev_proj_ev] * 4
        ptR = [None, None]
        avR = [prev_proj_ev, prev_proj_ev]
        smR = prev_proj_ev
        for hidx, h in enumerate([1, 0] + list(range(2, HEADS))):
            hs = s * HEADS + hidx
            par = hs % 2
            gq = [hs_end[hs - 2]["pe"], hs_end[hs - 2]["vv"]] if hs - 2 in hs_end or (hs >= 2) else []
            guard = ([("pe", hs_end[hs - 2]["pe"]), ("vv", hs_end[hs - 2]["vv"])]
                     if hs >= 2 else [])
            vguard = ([("pe", hs_end[hs - 1]["pe"]), ("vv", hs_end[hs - 1]["vv"])]
                      if hs >= 1 else [])
            lds = {}
            for nm, mm_, gg_ in (("q", h, guard), ("k", 16 + h, []), ("v", 32 + h, vguard)):
                base = {"q": a_q[par], "k": a_k[par], "v": a_v}[nm]
                for ti in range(4):
                    lds[(nm, ti)] = b.dma("sync", rbf(base + ti * 512, 512),
                                          qkv_s[mm_, 4 * s + ti],
                                          waits=gg_ if ti == 0 else ())
            b.wait("tensor", "qS", lds[("v", 3)][1])
            vt_ev = {}
            for jc in range(16):
                pp = jc % 4
                if stR[pp]:
                    b.waits("tensor", [stR[pp]])
                if jc >= 2:
                    b.waits("tensor", [vt_ev[jc - 2]])
                vpe = b.op("tensor", lambda e, jc=jc, pp=pp: e.matmul(
                    st2[pp][:, 0:128], rbf(a_v + jc * 128, 128), identb[:],
                    start=True, stop=True), "pe")
                stR[pp] = None
                eng = "scalar" if jc % 2 == 0 else "vector"
                sig = "ss" if eng == "scalar" else "vv"
                b.wait(eng, "pe", vpe[1])
                vt_ev[jc] = b.op(eng, lambda e, jc=jc, pp=pp: _cp(e, 
                    rbf(a_vtok + jc * 128, 128), st2[pp][:, 0:128]), sig)
                stR[pp] = vt_ev[jc]
            last_vt_ev = vt_ev[15]

            b.wait("scalar", "qS", lds[("q", 3)][1])
            sq_c = b.op("scalar", lambda e: e.activation(rbf(a_sq, 2048), rbf(a_q[par], 2048),
                                                         AF.Square), "ss")
            b.wait("tensor", "ss", sq_c[1])
            if smR:
                b.waits("tensor", [smR])
            qn = [None, None]
            for half in range(2):
                b.op("tensor", lambda e, half=half: e.matmul(
                    sm2[half][0:1, :], ones_cb[:], rbf(a_sq + half * 1024, 512),
                    start=True, stop=False))
                qn[half] = b.op("tensor", lambda e, half=half: e.matmul(
                    sm2[half][0:1, :], ones_cb[:], rbf(a_sq + 1024 + half * 512, 512),
                    start=False, stop=True), "pe")
            b.wait("vector", "pe", qn[1][1])
            b.chain("vector", lambda e: e.reduce_max(rows[0:1, 1024:1025], sm2[0][0:1, :], axis=AX.X))
            mq = b.chain("vector", lambda e: e.reduce_max(rows[0:1, 1040:1041], sm2[1][0:1, :],
                                                          axis=AX.X))
            b.wait("scalar", "qS", lds[("k", 3)][1])
            b.wait("scalar", "vv", mq[1])
            sqk = b.op("scalar", lambda e: e.activation(rbf(a_sq, 2048), rbf(a_k[par], 2048),
                                                        AF.Square), "ss")
            b.wait("tensor", "ss", sqk[1])
            b.wait("tensor", "vv", mq[1])
            kn = [None, None]
            for half in range(2):
                b.op("tensor", lambda e, half=half: e.matmul(
                    sm2[half][0:1, :], ones_cb[:], rbf(a_sq + half * 1024, 512),
                    start=True, stop=False))
                kn[half] = b.op("tensor", lambda e, half=half: e.matmul(
                    sm2[half][0:1, :], ones_cb[:], rbf(a_sq + 1024 + half * 512, 512),
                    start=False, stop=True), "pe")
            b.wait("vector", "pe", kn[1][1])
            b.chain("vector", lambda e: e.reduce_max(rows[0:1, 1056:1057], sm2[0][0:1, :], axis=AX.X))
            b.chain("vector", lambda e: e.reduce_max(rows[0:1, 1072:1073], sm2[1][0:1, :], axis=AX.X))
            b.chain("vector", lambda e: e.tensor_tensor(rows[0:1, 1024:1025], rows[0:1, 1024:1025],
                                                        rows[0:1, 1040:1041], ALU.max))
            b.chain("vector", lambda e: e.tensor_tensor(rows[0:1, 1056:1057], rows[0:1, 1056:1057],
                                                        rows[0:1, 1072:1073], ALU.max))
            b.chain("vector", lambda e: e.tensor_tensor(rows[0:1, 1088:1089],
                                                        rows[0:1, 1024:1025],
                                                        rows[0:1, 1056:1057], ALU.mult))
            m2c = b.chain("vector", lambda e: e.tensor_scalar(rows[0:1, 1088:1089],
                                                              rows[0:1, 1088:1089], 1e-12,
                                                              None, ALU.max))
            b.wait("scalar", "vv", m2c[1])
            b.chain("scalar", lambda e: e.activation(rows[0:1, 1104:1105], rows[0:1, 1088:1089],
                                                     AF.Ln))
            cc = b.chain("scalar", lambda e: e.activation(rows[0:1, 1120:1121], rows[0:1, 1104:1105],
                                                          AF.Exp, scale=0.5))
            b.wait("vector", "ss", cc[1])
            ncr = b.chain("vector", lambda e: e.tensor_scalar(rows[0:1, 1136:1137],
                                                              rows[0:1, 1120:1121], -1.0, None,
                                                              ALU.mult))
            b.wait("tensor", "vv", ncr[1])
            nb = b.op("tensor", lambda e: e.matmul(sm2[0][:, 0:1], ones_rf[:],
                                                   rows[0:1, 1136:1137],
                                                   start=True, stop=True), "pe")
            b.wait("vector", "pe", nb[1])
            negc = b.op("vector", lambda e: _cp(e, smalls[:, 320:321], sm2[0][:, 0:1]), "vv")
            smR = negc

            SKEW = 2
            for ig in range(4):
                igp = ig % 2
                njc = 4 * ig + 4
                exp_d = {}
                zg_d = {}
                for step in range(njc + SKEW):
                    jc = step - SKEW
                    if 0 <= jc < njc:
                        ptp = jc % 2
                        b.wait("tensor", "ss", exp_d[jc][1])
                        if jc in zg_d:
                            b.wait("tensor", "vv", zg_d[jc][1])
                        if jc == 0:
                            if avR[igp]:
                                b.waits("tensor", [avR[igp]])
                            if hs == 0 and ig == 0:
                                b.wait("tensor", "qS", spill[1])
                        b.waits("tensor", [vt_ev[jc]])
                        b.op("tensor", lambda e, jc=jc, igp=igp, ptp=ptp: e.matmul(
                            av2[igp], rbf(a_vtok + jc * 128, 128), rbf(a_PT[ptp], 512),
                            start=(jc == 0), stop=(jc == njc - 1)))
                        lmm = b.op("tensor", lambda e, jc=jc, igp=igp, ptp=ptp: e.matmul(
                            l2[igp], ones_cb[:], rbf(a_PT[ptp], 512),
                            start=(jc == 0), stop=(jc == njc - 1)), "pe")
                        ptR[ptp] = lmm
                    jc = step
                    if jc < njc:
                        tloc = jc - 4 * ig
                        ioff = 128 * tloc if tloc >= 0 else 0
                        stp = jc % 4
                        ptp = jc % 2
                        if stR[stp]:
                            b.waits("tensor", [stR[stp]])
                        smm = b.op("tensor", lambda e, jc=jc, ig=ig, stp=stp, ioff=ioff,
                                   par=par: e.matmul(
                                       st2[stp][:, ioff:512], rbf(a_k[par] + jc * 128, 128),
                                       rbf(a_q[par] + ig * 512 + ioff, 512 - ioff),
                                       start=True, stop=True), "pe")
                        if tloc >= 0:
                            b.wait("vector", "pe", smm[1])
                            mk = b.op("vector", lambda e, stp=stp, ioff=ioff: e.tensor_tensor(
                                st2[stp][:, ioff:ioff + 128], st2[stp][:, ioff:ioff + 128],
                                maskT[:], ALU.add), "vv")
                            b.wait("scalar", "vv", mk[1])
                        else:
                            b.wait("scalar", "pe", smm[1])
                        if tloc >= 1:
                            if ptR[ptp]:
                                b.waits("vector", [ptR[ptp]])
                            zg_d[jc] = b.op("vector", lambda e, ptp=ptp, ioff=ioff: e.memset(
                                rbf(a_PT[ptp], ioff), 0.0), "vv")
                        if ptR[ptp]:
                            b.waits("scalar", [ptR[ptp]])
                        b.wait("scalar", "vv", negc[1])
                        exp_d[jc] = b.op("scalar", lambda e, stp=stp, ptp=ptp, ioff=ioff:
                                         e.activation(rbf(a_PT[ptp] + ioff, 512 - ioff),
                                                      st2[stp][:, ioff:512], AF.Exp,
                                                      bias=smalls[:, 320:321]), "ss")
                        stR[stp] = exp_d[jc]
                b.wait("vector", "pe", ptR[(njc - 1) % 2][1])
                b.chain("vector", lambda e, igp=igp: e.tensor_scalar(
                    rows[0:1, 0:512], l2[igp], 1e-35, None, ALU.add))
                rr = b.chain("vector", lambda e: e.reciprocal(rows[0:1, 512:1024],
                                                              rows[0:1, 0:512]))
                if smR:
                    b.waits("tensor", [smR])
                b.wait("tensor", "vv", rr[1])
                rbp = b.op("tensor", lambda e, igp=igp: e.matmul(
                    sm2[igp], ones_rf[:], rows[0:1, 512:1024], start=True, stop=True), "pe")
                b.wait("vector", "pe", rbp[1])
                rbc = b.op("vector", lambda e, igp=igp: _cp(e, rf32(a_rb, 512),
                                                                      sm2[igp][:]), "vv")
                ave = b.op("vector", lambda e, igp=igp, h=h, ig=ig: e.tensor_tensor(
                    rbf(a_attnoT + h * 2048 + ig * 512, 512), av2[igp], rf32(a_rb, 512),
                    ALU.mult), "vv")
                avR[igp] = ave
                smR = rbc
            hs_end[hs] = {"pe": b.cnt["pe"], "vv": b.cnt["vv"], "ss": b.cnt["ss"]}

        if debug:
            dbg_aot = nc.dram_tensor(f"dbg_aot{s}", [128, 32768], BF16,
                                     kind="ExternalOutput").ap()
            b.dma("sync", dbg_aot, rbf(a_attnoT, 32768),
                  waits=[("vv", hs_end[s * HEADS + HEADS - 1]["vv"])])
        bld = b.dma("sync", rf32(a_bb, 2048)[0:1, :], bout_h,
                    waits=[("vv", hs_end[s * HEADS + HEADS - 1]["vv"])])
        b.wait("tensor", "qS", bld[1])
        b.wait("tensor", "qG", wout_ld[1])
        if smR:
            b.waits("tensor", [smR])
        for ig in range(2):
            if avR[ig]:
                b.waits("tensor", [avR[ig]])
        bbc_l = [None] * 4
        for n in range(4):
            if n >= 2:
                b.wait("tensor", "vv", bbc_l[n - 2][1])
            bbp = b.op("tensor", lambda e, n=n: e.matmul(
                sm2[n % 2][:], ones_rf[:], rf32(a_bb, 2048)[0:1, n * 512:(n + 1) * 512],
                start=True, stop=True), "pe")
            b.wait("vector", "pe", bbp[1])
            bbc_l[n] = b.op("vector", lambda e, n=n: _cp(e,
                rf32(a_bb + 1024 * n, 512), sm2[n % 2][:]), "vv")
        bbc = bbc_l[3]
        es_a.close()
        es_p = ExitStack()
        pj = [es_p.enter_context(nc.psum_tensor(f"pj{s}_{i}", [128, 2048], F32)) for i in range(2)]
        pj_ev = [None, None]
        for t in range(16):
            tpar = t % 2
            gidx = s * 16 + t
            if t == 0:
                b.wait("tensor", "vv", bbc[1])
            if pj_ev[tpar]:
                b.waits("tensor", [pj_ev[tpar]])
            pj_pe = None
            for k in range(KC):
                for n in range(4):
                    last = (k == KC - 1 and n == 3)
                    r = b.op("tensor", lambda e, k=k, n=n, t=t, tpar=tpar: e.matmul(
                        pj[tpar][:, n * 512:(n + 1) * 512],
                        rbf(a_attnoT + k * 2048 + t * 128, 128),
                        rbf(a_wout + k * 2048 + n * 512, 512),
                        start=(k == 0), stop=(k == KC - 1)), "pe" if last else None)
                    if last:
                        pj_pe = r
            b.wait("vector", "pe", pj_pe[1])
            if gidx >= 2:
                b.wait("vector", "qS", proj_store[gidx - 2][1])
            pj_ev[tpar] = b.op("vector", lambda e, tpar=tpar: e.tensor_tensor(
                rf32(a_sst[tpar], 2048), pj[tpar][:], rf32(a_bb, 2048), ALU.add), "vv")
            proj_store[gidx] = b.dma("sync", s_s[gidx], rf32(a_sst[tpar], 2048),
                                     waits=[pj_ev[tpar]])
        if debug:
            dbg_bb = nc.dram_tensor(f"dbg_bb{s}", [128, 2048], F32,
                                    kind="ExternalOutput").ap()
            b.dma("sync", dbg_bb, rf32(a_bb, 2048), waits=[pj_ev[1]])
        prev_proj_ev = pj_ev[1]
        es_p.close()

    if upto == "proj":
        b.wait("sync", "qS", proj_store[NT128 - 1][1])
        blk = es.enter_context(nc.Block())
        b.emit_all(blk)
        es.close()
        return nc

    q_chunk = [0, 8192]
    q_aa = 16384
    q_qq = [24576, 26624]
    q_part = 28672

    es_qz = ExitStack()
    qt_ps = es_qz.enter_context(nc.psum_tensor("qt_ps", [128, 512], F32))
    lp_vv = b.cnt["vv"]
    lp_st = proj_store[NT128 - 1]
    last_q_store = None
    for wi in range(3):
        wh = (w1_h, w2_h, w3_h)[wi]
        qs_ = qw_s[wi]
        flat = wh.rearrange("a b c d -> (a b) (c d)") if wi < 2 else wh
        if wi == 0:
            b.waits("gpsimd", [("vv", lp_vv), ("qS", lp_st[1])])
        else:
            b.wait("gpsimd", "pe", gs[1])
        zq = b.op("gpsimd", lambda e: e.memset(rf32(q_part, 720), 0.0), "gg")
        b.wait("vector", "gg", zq[1])
        red = {}
        for ci in range(MCF):
            par = ci % 2
            if wi == 0 and ci < 2:
                w = [("vv", lp_vv), ("qS", lp_st[1])]
            elif ci < 2:
                w = [("vv", qq_d[MCF - 2 + ci][1])]
            else:
                w = [("vv", red[ci - 2][1])]
            ld = b.dma("sync", rf32(q_chunk[par], 2048), flat[ci * 128:(ci + 1) * 128, :],
                       waits=w)
            b.wait("vector", "qS", ld[1])
            red[ci] = b.op("vector", lambda e, ci=ci, par=par: e.tensor_reduce(
                rf32(q_part, 720)[:, 16 * ci:16 * ci + 1], rf32(q_chunk[par], 2048), AX.X,
                ALU.add, apply_absolute_value=True), "vv")
        b.wait("vector", "vv", red[MCF - 1][1])
        p1 = b.chain("vector", lambda e: e.reduce_sum(rf32(q_part, 720)[:, 704:705],
                                                      rf32(q_part, 720)[:, 0:704], axis=AX.X))
        b.wait("tensor", "vv", p1[1])
        gs = b.op("tensor", lambda e: e.matmul(qt_ps[0:1, 0:1], rf32(q_part, 720)[:, 704:705],
                                               ones_cf[:], start=True, stop=True), "pe")
        b.wait("vector", "pe", gs[1])
        b.chain("vector", lambda e: e.tensor_scalar(rows[0:1, 1168:1169], qt_ps[0:1, 0:1],
                                                    0.5 / N_W, None, ALU.mult))
        thr = b.chain("vector", lambda e: e.tensor_scalar(rows[0:1, 1168:1169],
                                                          rows[0:1, 1168:1169], 0.5e-8, None,
                                                          ALU.add))
        b.wait("tensor", "vv", thr[1])
        thb = b.op("tensor", lambda e: e.matmul(qt_ps[:, 0:1], ones_rf[:],
                                                rows[0:1, 1168:1169], start=True, stop=True), "pe")
        b.wait("vector", "pe", thb[1])
        thbc = b.op("vector", lambda e: _cp(e, smalls[:, 336:337], qt_ps[:, 0:1]), "vv")
        qq_d = {}
        qst = {}
        for ci in range(MCF):
            par = ci % 2
            w = [("vv", thbc[1])] if ci < 2 else [("vv", qq_d[ci - 2][1])]
            ld = b.dma("sync", rf32(q_chunk[par], 2048), flat[ci * 128:(ci + 1) * 128, :],
                       waits=w)
            b.wait("vector", "qS", ld[1])
            if ci >= 2:
                b.wait("vector", "qS", qst[ci - 2][1])
            b.op("vector", lambda e, par=par: e.tensor_scalar(
                rf32(q_aa, 2048), rf32(q_chunk[par], 2048), smalls[:, 336:337], 0.0,
                ALU.add, ALU.is_lt))
            qq_d[ci] = b.op("vector", lambda e, par=par: e.scalar_tensor_tensor(
                rbf(q_qq[par], 2048), rf32(q_chunk[par], 2048), smalls[:, 336:337],
                rf32(q_aa, 2048), ALU.is_gt, ALU.subtract), "vv")
            qst[ci] = b.dma("sync", qs_[ci], rbf(q_qq[par], 2048), waits=[qq_d[ci]])
        last_q_store = qst[MCF - 1]
    es_qz.close()

    f_gsb = 0
    f_xnT = 32768
    f_hid = 49152
    f_qw1 = [71680, 73728]
    f_qw2 = [75776, 77824]
    f_qw3 = [79872, 80896]
    f_sig = [83968, 84992]
    f_st = 86016
    f_xt = 90112
    f_sgm = [94208, 96256]

    es_f = ExitStack()
    ps4 = [es_f.enter_context(nc.psum_tensor(f"fps{i}", [128, 1024], F32)) for i in range(4)]

    dbg_hid = nc.dram_tensor("dbg_hid", [MCF, 128, 1024], BF16, kind=dk).ap() if debug else None
    dbg_gsb = nc.dram_tensor("dbg_gsb", [8, 128, 2048], F32, kind=dk).ap() if debug else None
    ffn3_pe = None
    prev_gev = None
    prev_gmul = None
    prev_fin = None
    sgm_rd = [None, None]
    out_store = [None] * NT128
    hid_d = {}
    x12_d = {}
    g_ev = {}

    for tt in range(4):
        s, tl = divmod(tt, 2)
        xn_lds = None
        for k in range(KC):
            w = []
            if k == 0:
                w = [("pe", (x12_d[(tt - 1, 1, 21)][1] if tt > 0 else 0))]
                if tt == 0:
                    w.append(("qS", last_q_store[1]))
            xn_lds = b.dma("sync", rbf(f_xnT + k * 1024, 1024),
                           xnT_s[k][:, tt * 1024:(tt + 1) * 1024], waits=w)
        for half in range(2):
            mbase = 22 * half
            qw12_ld = {}
            for ml in range(23):
                if ml < 22:
                    m = mbase + ml
                    par = ml % 2
                    if ml >= 2:
                        w1w = [("pe", x12_d[(tt, half, ml - 2)][1])]
                    elif tt == 0 and half == 0:
                        w1w = [("qS", last_q_store[1])]
                    else:
                        w1w = ()
                    b.dma("gpsimd", rbf(f_qw1[par], 2048), qw_s[0][m], waits=w1w)
                    qw12_ld[ml] = b.dma("gpsimd", rbf(f_qw2[par], 2048), qw_s[1][m])
                if ml == 0:
                    continue
                ml, m, par = ml - 1, mbase + ml - 1, (ml - 1) % 2
                b.wait("tensor", "qG", qw12_ld[ml][1])
                if ml == 0 and half == 0:
                    b.wait("tensor", "qS", xn_lds[1])
                if ml < 2:
                    if prev_gev:
                        b.waits("tensor", [prev_gev])
                elif ml >= 2:
                    b.wait("tensor", "vv", hid_d[(tt, half, ml - 2)][1])
                for k in range(KC):
                    for n in range(2):
                        b.op("tensor", lambda e, par=par, k=k, n=n: e.matmul(
                            ps4[par][:, n * 512:(n + 1) * 512], rbf(f_qw1[par] + k * 128, 128),
                            rbf(f_xnT + k * 1024 + n * 512, 512),
                            start=(k == 0), stop=(k == KC - 1)))
                for k in range(KC):
                    for n in range(2):
                        last = (k == KC - 1 and n == 1)
                        r = b.op("tensor", lambda e, par=par, k=k, n=n: e.matmul(
                            ps4[2 + par][:, n * 512:(n + 1) * 512], rbf(f_qw2[par] + k * 128, 128),
                            rbf(f_xnT + k * 1024 + n * 512, 512),
                            start=(k == 0), stop=(k == KC - 1)), "pe" if last else None)
                        if last:
                            x12_d[(tt, half, ml)] = r
                b.wait("scalar", "pe", x12_d[(tt, half, ml)][1])
                if ml >= 2:
                    b.wait("scalar", "vv", hid_d[(tt, half, ml - 2)][1])
                sg = b.op("scalar", lambda e, par=par: e.activation(
                    rbf(f_sig[par], 1024), ps4[par][:], AF.Sigmoid), "ss")
                b.wait("vector", "ss", sg[1])
                hid_d[(tt, half, ml)] = b.op("vector", lambda e, par=par, ml=ml: e.tensor_tensor(
                    rbf(f_hid + ml * 1024, 1024), ps4[2 + par][:], rbf(f_sig[par], 1024),
                    ALU.mult), "vv")
                if debug and tt == 0:
                    b.dma("sync", dbg_hid[mbase + ml], rbf(f_hid + ml * 1024, 1024),
                          waits=[hid_d[(tt, half, ml)]])
            b.wait("tensor", "vv", hid_d[(tt, half, 21)][1])
            for quad in range(2):
                for nh in range(2):
                    ml_pe = {}
                    qw3_ld = {}
                    for ml in range(23):
                        if ml < 22:
                            m = mbase + ml
                            par = ml % 2
                            if ml >= 2:
                                w = [("pe", ml_pe[ml - 2][1])]
                            elif ffn3_pe is not None:
                                w = [("pe", ffn3_pe[1])]
                            else:
                                w = [("qS", last_q_store[1])]
                            qw3_ld[ml] = b.dma("gpsimd", rbf(f_qw3[par], 1024),
                                               qw_s[2][m][:, nh * 1024:(nh + 1) * 1024],
                                               waits=w)
                        if ml == 0:
                            continue
                        ml, m, par = ml - 1, mbase + ml - 1, (ml - 1) % 2
                        b.wait("tensor", "qG", qw3_ld[ml][1])
                        if ml == 0 and prev_gev:
                            b.waits("tensor", [prev_gev])
                        for t4 in range(4):
                            t128 = quad * 4 + t4
                            for nn in range(2):
                                lastml = (t4 == 3 and nn == 1)
                                r = b.op("tensor", lambda e, ml=ml, par=par, t4=t4, t128=t128,
                                         nn=nn: e.matmul(
                                             ps4[t4][:, nn * 512:(nn + 1) * 512],
                                             rbf(f_hid + ml * 1024 + t128 * 128, 128),
                                             rbf(f_qw3[par] + nn * 512, 512),
                                             start=(ml == 0), stop=(ml == 21)),
                                         "pe" if lastml else None)
                                if lastml:
                                    ml_pe[ml] = r
                    ffn3_pe = ml_pe[21]
                    for t4 in range(4):
                        t128 = quad * 4 + t4
                        dst = rf32(f_gsb + t128 * 4096 + nh * 2048, 1024)
                        b.wait("vector", "pe", ffn3_pe[1])
                        if half == 0 and tt > 0:
                            b.wait("vector", "qS", out_store[(tt - 1) * 8 + t128][1])
                        if half == 0:
                            ev = b.op("vector", lambda e, t4=t4, dst=dst: _cp(e, 
                                dst, ps4[t4][:]), "vv")
                        else:
                            ev = b.op("vector", lambda e, t4=t4, dst=dst: e.tensor_tensor(
                                dst, dst, ps4[t4][:], ALU.add), "vv")
                            g_ev[(tt, t128, nh)] = ev
                    prev_gev = ev
        dbg_gd = None
        if debug and tt == 0:
            for t128 in range(8):
                dbg_gd = b.dma("sync", dbg_gsb[t128], rf32(f_gsb + t128 * 4096, 2048),
                               waits=[g_ev[(tt, t128, 0)], g_ev[(tt, t128, 1)]])
        for t128 in range(8):
            gt = tt * 8 + t128
            trow = tl * 8 + t128
            gsb = rf32(f_gsb + t128 * 4096, 2048)
            sld = b.dma("sync", rf32(f_st, 2048), s_s[s * 16 + trow],
                        waits=([g_ev[(tt, t128, 1)]] if t128 == 0 else [])
                        + ([prev_gmul] if prev_gmul else []))
            xld = b.dma("sync", rf32(f_xt, 2048), x_h[s, trow * 128:(trow + 1) * 128, :],
                        waits=[prev_fin] if prev_fin else [])
            spar = t128 % 2
            b.waits("scalar", [g_ev[(tt, t128, 0)], g_ev[(tt, t128, 1)]])
            if sgm_rd[spar]:
                b.wait("scalar", "gg", sgm_rd[spar][1])
            if dbg_gd and t128 == 0:
                b.wait("scalar", "qS", dbg_gd[1])
            sgc = b.op("scalar", lambda e, gsb=gsb, spar=spar: e.activation(
                rbf(f_sgm[spar], 2048), gsb, AF.Sigmoid), "ss")
            b.wait("gpsimd", "ss", sgc[1])
            sgm_rd[spar] = b.chain("gpsimd", lambda e, gsb=gsb, spar=spar: e.tensor_tensor(
                gsb, gsb, rbf(f_sgm[spar], 2048), ALU.mult))
            b.wait("gpsimd", "qS", sld[1])
            gmul = b.chain("gpsimd", lambda e, gsb=gsb: e.tensor_tensor(
                gsb, gsb, rf32(f_st, 2048), ALU.mult))
            prev_gmul = gmul
            b.wait("vector", "gg", gmul[1])
            b.wait("vector", "qS", xld[1])
            fin = b.chain("vector", lambda e, gsb=gsb: e.tensor_tensor(
                gsb, gsb, rf32(f_xt, 2048), ALU.add))
            prev_fin = fin
            out_store[gt] = b.dma("sync", out_h[s, trow * 128:(trow + 1) * 128, :], gsb,
                                  waits=[fin])
    es_f.close()

    b.wait("sync", "qS", out_store[NT128 - 1][1])
    b.wait("gpsimd", "gg", g_ones[1])

    blk = es.enter_context(nc.Block())
    b.emit_all(blk)
    es.close()
    return nc


def _prep_weights(wqkv, wout, bout, w1, w2, w3):
    wqkv = np.ascontiguousarray(
        np.asarray(wqkv, dtype=np.float32).reshape(KC, 128, MCQ, 128).transpose(2, 1, 0, 3))
    wout = np.ascontiguousarray(np.asarray(wout, dtype=np.float32))
    bout = np.ascontiguousarray(np.asarray(bout, dtype=np.float32).reshape(1, D))
    w1 = np.ascontiguousarray(
        np.asarray(w1, dtype=np.float32).T.reshape(KC, 128, MCF, 128).transpose(2, 1, 0, 3))
    w2 = np.ascontiguousarray(
        np.asarray(w2, dtype=np.float32).T.reshape(KC, 128, MCF, 128).transpose(2, 1, 0, 3))
    w3 = np.ascontiguousarray(np.asarray(w3, dtype=np.float32).T)
    return wqkv, wout, bout, w1, w2, w3


def kernel(L0_attn_x, L0_attn_w_qkv, L0_attn_w_out, L0_attn_b_out,
           L0_ffn_w1, L0_ffn_w2, L0_ffn_w3):
    x = np.ascontiguousarray(np.asarray(L0_attn_x, dtype=np.float32))
    wqkv, wout, bout, w1, w2, w3 = _prep_weights(
        L0_attn_w_qkv, L0_attn_w_out, L0_attn_b_out, L0_ffn_w1, L0_ffn_w2, L0_ffn_w3)
    nc = build_program()
    in_maps = [{"x": np.ascontiguousarray(x[2 * c:2 * c + 2]),
                "wqkv": wqkv, "wout": wout, "bout": bout,
                "w1": w1, "w2": w2, "w3": w3} for c in range(8)]
    res = run_bass_kernel_spmd(nc, in_maps, core_ids=list(range(8)))
    return np.concatenate([r["out"] for r in res.results], axis=0)
